# version 02 (final)
# baseline (speedup 1.0000x reference)
import math
from contextlib import ExitStack

import numpy as np
import concourse.bass as bass
import concourse.mybir as mybir
from concourse.bass_utils import run_bass_kernel_spmd

F32 = mybir.dt.float32
BF16 = mybir.dt.bfloat16
AF = mybir.ActivationFunctionType
ALU = mybir.AluOpType
AX = mybir.AxisListType

ENGINES = ("tensor", "vector", "scalar", "gpsimd", "sync")
DMA_QUEUES = ("sync", "gpsimd", "scalar")

B, S, D = 4, 4096, 1024
NCHUNK = D // 128


class _Op:
    __slots__ = ("eng", "fn", "deps", "dma", "idx", "inc_needed", "count",
                 "slot", "use_idx")

    def __init__(self, eng, fn, dma, idx):
        self.eng, self.fn, self.dma, self.idx = eng, fn, dma, idx
        self.deps = set()
        self.inc_needed = False
        self.count = None
        self.slot = None
        self.use_idx = 0


NDMASLOT = 8


class View:
    __slots__ = ("tile", "p0", "p1", "c0", "c1", "rawap")

    def __init__(self, tile, p0, p1, c0, c1, rawap=None):
        self.tile, self.p0, self.p1, self.c0, self.c1 = tile, p0, p1, c0, c1
        self.rawap = rawap

    @property
    def ap(self):
        if self.rawap is not None:
            return self.rawap
        return self.tile.h[self.p0:self.p1, self.c0:self.c1]

    @property
    def shape(self):
        return (self.p1 - self.p0, self.c1 - self.c0)

    def covers(self, o):
        if self.tile.whole:
            return True
        return (self.p0 <= o.p0 and self.p1 >= o.p1
                and self.c0 <= o.c0 and self.c1 >= o.c1)

    def overlaps(self, o):
        if self.tile is not o.tile:
            return False
        if self.tile.whole:
            return True
        return (self.p0 < o.p1 and o.p0 < self.p1
                and self.c0 < o.c1 and o.c0 < self.c1)


class Tile:
    def __init__(self, prog, h, parts, free, dtype, whole_conflict):
        self.prog, self.h = prog, h
        self.parts, self.free, self.dtype = parts, free, dtype
        self.whole = whole_conflict
        self.events = []

    def __getitem__(self, idx):
        ps, cs = idx if isinstance(idx, tuple) else (idx, slice(None))
        p0, p1, _ = ps.indices(self.parts)
        c0, c1, _ = cs.indices(self.free)
        return View(self, p0, p1, c0, c1)

    @property
    def full(self):
        return View(self, 0, self.parts, 0, self.free)

    def rawv(self, ap, p0=0, p1=None, c0=0, c1=None):
        return View(self, p0, self.parts if p1 is None else p1,
                    c0, self.free if c1 is None else c1, rawap=ap)


class DramT:

    def __init__(self, prog, h, shape, name):
        self.prog, self.h, self.shape, self.name = prog, h, shape, name
        self.events = []

    def __getitem__(self, idx):
        return self.h[idx]

    @property
    def ap(self):
        return self.h.ap() if hasattr(self.h, "ap") else self.h


class Prog:
    def __init__(self, name, n_cores=8):
        self.name = name
        self.n_cores = n_cores
        self.nc = bass.Bass("TRN2", target_bir_lowering=False)
        self.stack = ExitStack()
        self.ops = []
        self.inputs = {}
        self.consts = {}
        self.outputs = {}
        self.dram = {}
        self.sbuf_bytes = 0
        self.psum_banks = 0
        self._sealed = False
        self._tiles = {}
        self._dma_slot_rr = {q: 0 for q in DMA_QUEUES}
        self._dma_slot_last = {}
        self._dma_slot_uses = {}

    def inp(self, name, shape, dtype=F32):
        h = self.nc.dram_tensor(name, list(shape), dtype, kind="ExternalInput").ap()
        self.inputs[name] = tuple(shape)
        t = DramT(self, h, tuple(shape), name)
        self.dram[name] = t
        return t

    def const(self, name, arr):
        import ml_dtypes
        arr = np.ascontiguousarray(arr)
        dt = {np.dtype(np.float32): F32, np.dtype(np.int32): mybir.dt.int32,
              np.dtype(ml_dtypes.bfloat16): BF16}[arr.dtype]
        h = self.nc.dram_tensor(name, list(arr.shape), dt, kind="ExternalInput").ap()
        self.consts[name] = arr
        t = DramT(self, h, arr.shape, name)
        self.dram[name] = t
        return t

    def out(self, name, shape, dtype=F32):
        h = self.nc.dram_tensor(name, list(shape), dtype, kind="ExternalOutput").ap()
        self.outputs[name] = tuple(shape)
        t = DramT(self, h, tuple(shape), name)
        self.dram[name] = t
        return t

    def dram_scratch(self, name, shape, dtype=F32):
        h = self.nc.dram_tensor(name, list(shape), dtype, kind="Internal").ap()
        t = DramT(self, h, tuple(shape), name)
        self.dram[name] = t
        return t

    def sb(self, name, parts, free, dtype=F32):
        esz = 4 if dtype in (F32, mybir.dt.int32) else 2
        self.sbuf_bytes += free * esz
        assert self.sbuf_bytes < 206 * 1024, \
            f"SBUF budget blown at {name}: {self.sbuf_bytes/1024:.0f} KiB/partition"
        h = self.stack.enter_context(self.nc.sbuf_tensor(name, [parts, free], dtype))
        t = Tile(self, h, parts, free, dtype, False)
        self._tiles[name] = t
        return t

    def ps(self, name, parts=128, free=512, dtype=F32):
        esz = 4 if dtype == F32 else 2
        assert free * esz <= 2048, "psum tile must fit one bank"
        self.psum_banks += 1
        assert self.psum_banks <= 8, "more than 8 psum banks"
        h = self.stack.enter_context(self.nc.psum_tensor(name, [parts, free], dtype))
        t = Tile(self, h, parts, free, dtype, True)
        self._tiles[name] = t
        return t

    def _record(self, eng, fn, reads, writes, dma=False):
        op = _Op(eng, fn, dma, len(self.ops))
        if dma:
            slot = self._dma_slot_rr[eng]
            self._dma_slot_rr[eng] = (slot + 1) % NDMASLOT
            op.slot = slot
            key = (eng, slot)
            prev = self._dma_slot_last.get(key)
            if prev is not None:
                op.deps.add(prev)
            self._dma_slot_last[key] = op
            self._dma_slot_uses[key] = self._dma_slot_uses.get(key, 0) + 1
            op.use_idx = self._dma_slot_uses[key]
        for v in reads:
            if isinstance(v, View):
                for (o, kind, r) in v.tile.events:
                    if kind == "W" and r.overlaps(v):
                        op.deps.add(o)
            elif isinstance(v, tuple):
                for o in v[0].events:
                    op.deps.add(o)
        for v in writes:
            if isinstance(v, View):
                for (o, kind, r) in v.tile.events:
                    if r.overlaps(v):
                        op.deps.add(o)
        for v in reads:
            if isinstance(v, View):
                v.tile.events.append((op, "R", v))
        for v in writes:
            if isinstance(v, View):
                ev = v.tile.events
                keep = [e for e in ev if not (v.covers(e[2]) and e[0] in op.deps
                                              or v.covers(e[2]) and e[0] is op)]
                keep.append((op, "W", v))
                v.tile.events[:] = keep
            elif isinstance(v, tuple):
                v[0].events.append(op)
        op.deps.discard(op)
        self.ops.append(op)
        return op

    def op(self, eng, method, writes, reads, *args, **kwargs):
        def fn(e):
            aa = [a.ap if isinstance(a, View) else a for a in args]
            kk = {k: (v.ap if isinstance(v, View) else v) for k, v in kwargs.items()}
            return getattr(e, method)(*aa, **kk)
        return self._record(eng, fn, reads, writes)

    def dma(self, queue, dst, src, **kwargs):
        reads, writes = [], []
        if isinstance(src, View):
            reads.append(src)
            src_ap = src.ap
        else:
            t, src_ap = src
            reads.append((t,))
        if isinstance(dst, View):
            writes.append(dst)
            dst_ap = dst.ap
        else:
            t, dst_ap = dst
            writes.append((t,))

        def fn(e):
            return e.dma_start(out=dst_ap, in_=src_ap, **kwargs)
        return self._record(queue, fn, reads, writes, dma=True)

    def mark(self, tag):
        if not hasattr(self, "sections"):
            self.sections = []
        self.sections.append((tag, len(self.ops)))

    def seal(self, paranoid=False, noexempt=False, serialize_tags=()):
        assert not self._sealed
        self._sealed = True
        self._noexempt = noexempt
        self._paranoid = paranoid
        if serialize_tags and hasattr(self, "sections"):
            bounds = self.sections + [("__end", len(self.ops))]
            for (tag, i0), (_, i1) in zip(bounds, bounds[1:]):
                if tag in serialize_tags or "ALL" in serialize_tags:
                    for i in range(max(i0, 1), i1):
                        self.ops[i].deps.add(self.ops[i - 1])
        if paranoid:
            for i, op in enumerate(self.ops):
                if i > 0:
                    op.deps.add(self.ops[i - 1])
        for op in self.ops:
            for d in op.deps:
                if (not paranoid and not noexempt
                        and d.eng == op.eng == "tensor"
                        and not d.dma and not op.dma):
                    continue
                d.inc_needed = True
        for d in self.ops:
            if d.dma:
                d.inc_needed = True
        counts = {e: 0 for e in ENGINES}
        for op in self.ops:
            if op.inc_needed:
                if op.dma:
                    op.count = 16 * op.use_idx
                else:
                    counts[op.eng] += 1
                    op.count = counts[op.eng]
        self.final_dma = {key: 16 * n for key, n in self._dma_slot_uses.items()}

        nc = self.nc
        sems = {}
        with ExitStack() as semstack:
            for e in ENGINES:
                sems[e] = semstack.enter_context(nc.semaphore(f"s_{e}"))
            for e in DMA_QUEUES:
                for sl in range(NDMASLOT):
                    sems[("D" + e, sl)] = semstack.enter_context(
                        nc.semaphore(f"d_{e}{sl}"))
            waited = {e: {} for e in ENGINES}

            with nc.Block() as block:
                def mk(eng_name):
                    def section(eng):
                        w = waited[eng_name]
                        for op in self.ops:
                            if op.eng != eng_name:
                                continue
                            for d in sorted(op.deps, key=lambda o: o.idx):
                                if (not self._paranoid and not self._noexempt
                                        and d.eng == eng_name == "tensor"
                                        and not d.dma and not op.dma):
                                    continue
                                key = ("D" + d.eng, d.slot) if d.dma else d.eng
                                if d.count is None:
                                    continue
                                if w.get(key, 0) >= d.count:
                                    continue
                                eng.wait_ge(sems[key], d.count)
                                w[key] = d.count
                            inst = op.fn(eng)
                            if op.inc_needed:
                                inst.then_inc(sems[("D" + op.eng, op.slot)] if op.dma
                                              else sems[op.eng], 16 if op.dma else 1)
                        if eng_name == "sync":
                            for key, v in sorted(self.final_dma.items()):
                                q, sl = key
                                eng.wait_ge(sems[("D" + q, sl)], v)
                    return section

                used = set(op.eng for op in self.ops) | {"sync"}
                for e in ENGINES:
                    if e in used:
                        getattr(block, e)(mk(e))

    def run(self, per_core_inputs):
        if not self._sealed:
            self.seal()
        in_maps = []
        for c in range(self.n_cores):
            m = dict(per_core_inputs[c])
            for k, v in self.consts.items():
                m[k] = v
            in_maps.append(m)
        res = run_bass_kernel_spmd(self.nc, in_maps,
                                   core_ids=list(range(self.n_cores)))
        self.stack.close()
        return res.results


def ceil_div(a, b):
    return (a + b - 1) // b


class Ctx:

    def __init__(self, k: Prog):
        self.k = k
        self.ones_col = k.sb("ones_col", 128, 2, BF16)
        self.ones_row = k.sb("ones_row", 1, 128, BF16)
        self.ones_row_f = k.sb("ones_row_f", 1, 128, F32)
        self.ones_col_f = k.sb("ones_col_f", 128, 2, F32)
        k.op("gpsimd", "memset", [self.ones_col.full], [], self.ones_col.full, 1.0)
        k.op("gpsimd", "memset", [self.ones_row.full], [], self.ones_row.full, 1.0)
        k.op("gpsimd", "memset", [self.ones_row_f.full], [], self.ones_row_f.full, 1.0)
        k.op("gpsimd", "memset", [self.ones_col_f.full], [], self.ones_col_f.full, 1.0)
        self._psum = {}

    def psum(self, name, parts=128, free=512, dtype=F32):
        if name not in self._psum:
            self._psum[name] = self.k.ps(name, parts, free, dtype)
        return self._psum[name]


def load_fm(k, dram_t, rows, cols, name, row0=0, col0=0, cast_bf16=True,
            eng="vector", queue="sync"):
    nch = ceil_div(rows, 128)
    f = k.sb(name + "_f", 128, nch * cols, F32)
    for ci in range(nch):
        r0 = row0 + ci * 128
        k.dma(queue, f[:, ci * cols:(ci + 1) * cols],
              (dram_t, dram_t[r0:r0 + 128, col0:col0 + cols]))
    bt = None
    if cast_bf16:
        bt = k.sb(name + "_b", 128, nch * cols, BF16)
        k.op(eng, "tensor_copy", [bt.full], [f.full], bt.full, f.full)
    return f, bt


def cast_chunks(k, dst, src, eng="vector", n=1):
    engs = ["vector", "scalar", "gpsimd"]
    W = src.free
    step = ceil_div(W, n)
    for i in range(n):
        c0, c1 = i * step, min(W, (i + 1) * step)
        if c0 >= c1:
            break
        e = eng if n == 1 else engs[i % len(engs)]
        if e == "scalar":
            k.op(e, "activation", [dst[:, c0:c1]], [src[:, c0:c1]],
                 dst[:, c0:c1], src[:, c0:c1], AF.Copy)
        else:
            k.op(e, "tensor_copy", [dst[:, c0:c1]], [src[:, c0:c1]],
                 dst[:, c0:c1], src[:, c0:c1])


class FM:

    def __init__(self, k, name, nrows, ncols, dtype=BF16):
        self.nch = ceil_div(nrows, 128)
        self.nrows, self.ncols, self.dtype = nrows, ncols, dtype
        self.tile = k.sb(name, 128, self.nch * ncols, dtype)

    def chunk(self, ci, c0=0, c1=None, p0=0, p1=128):
        c1 = self.ncols if c1 is None else c1
        W = self.ncols
        return self.tile[p0:p1, ci * W + c0:ci * W + c1]


def load_w_fm(k, dram_t, din, dout, fm_f32_scratch, fm_bf16, row0=0, col0=0,
              queue="sync", cast_eng="vector"):
    nch = ceil_div(din, 128)
    sts = fm_f32_scratch if isinstance(fm_f32_scratch, (list, tuple)) \
        else [fm_f32_scratch]
    W = sts[0].free
    i = 0
    for ci in range(nch):
        for c0 in range(0, dout, W):
            cw = min(W, dout - c0)
            st = sts[i % len(sts)]
            i += 1
            k.dma(queue, st[:, 0:cw],
                  (dram_t, dram_t[row0 + ci * 128:row0 + (ci + 1) * 128,
                                  col0 + c0:col0 + c0 + cw]))
            k.op(cast_eng, "tensor_copy", [fm_bf16.chunk(ci, c0, c0 + cw)],
                 [st[:, 0:cw]], fm_bf16.chunk(ci, c0, c0 + cw), st[:, 0:cw])


def proj_fm(k, ctx, w_fm, x_fm, dout, T, evac, nt=512, nch=None, x_c0=0,
            psum_names=("pA", "pB"), w_c0=0):
    nch = w_fm.nch if nch is None else nch
    i = 0
    for co0 in range(0, dout, 128):
        cow = min(128, dout - co0)
        for t0 in range(0, T, nt):
            tw = min(nt, T - t0)
            ps = ctx.psum(psum_names[i % len(psum_names)])
            i += 1
            pv = ps[0:cow, 0:tw]
            for ci in range(nch):
                k.op("tensor", "matmul", [pv],
                     [w_fm.chunk(ci, w_c0 + co0, w_c0 + co0 + cow),
                      x_fm.chunk(ci, x_c0 + t0, x_c0 + t0 + tw)],
                     pv, w_fm.chunk(ci, w_c0 + co0, w_c0 + co0 + cow),
                     x_fm.chunk(ci, x_c0 + t0, x_c0 + t0 + tw),
                     start=(ci == 0), stop=(ci == nch - 1))
            evac(pv, co0, cow, t0, tw)


def bcast_psum(k, ctx, row_view, tw, psname="pC"):
    ps = ctx.psum(psname)
    pv = ps[0:128, 0:tw]
    k.op("tensor", "matmul", [pv], [ctx.ones_row_f[0:1, 0:128], row_view],
         pv, ctx.ones_row_f[0:1, 0:128], row_view, start=True, stop=True)
    return pv


def emit_ln_rows(k, ctx, xb, TM, rows, sq_tiles, eps=1e-5, dsize=None,
                 psA="pC", psB="pD"):
    D_ = xb.nch * 128 if dsize is None else dsize
    nch = xb.nch
    for t0 in range(0, TM, 512):
        tw = min(512, TM - t0)
        pS = ctx.psum(psA)[0:1, 0:tw]
        pQ = ctx.psum(psB)[0:1, 0:tw]
        for ci in range(nch):
            k.op("tensor", "matmul", [pS],
                 [ctx.ones_col[:, 0:1], xb.chunk(ci, t0, t0 + tw)],
                 pS, ctx.ones_col[:, 0:1], xb.chunk(ci, t0, t0 + tw),
                 start=(ci == 0), stop=(ci == nch - 1))
        for ci in range(nch):
            sq = sq_tiles[ci % 2][:, 0:tw]
            k.op("scalar", "activation", [sq], [xb.chunk(ci, t0, t0 + tw)],
                 sq, xb.chunk(ci, t0, t0 + tw), AF.Square)
            k.op("tensor", "matmul", [pQ], [ctx.ones_col[:, 0:1], sq],
                 pQ, ctx.ones_col[:, 0:1], sq,
                 start=(ci == 0), stop=(ci == nch - 1))
        k.op("vector", "tensor_scalar", [rows[0:1, t0:t0 + tw]], [pS],
             rows[0:1, t0:t0 + tw], pS, 1.0 / D_, scalar2=None, op0=ALU.mult)
        k.op("vector", "tensor_scalar", [rows[0:1, TM + t0:TM + t0 + tw]], [pQ],
             rows[0:1, TM + t0:TM + t0 + tw], pQ, 1.0 / D_, scalar2=None,
             op0=ALU.mult)
    mean = rows[0:1, 0:TM]
    ex2 = rows[0:1, TM:2 * TM]
    tmp = rows[0:1, 2 * TM:3 * TM]
    k.op("vector", "tensor_tensor", [tmp], [mean], tmp, mean, mean, op=ALU.mult)
    k.op("vector", "tensor_tensor", [ex2], [ex2, tmp], ex2, ex2, tmp,
         op=ALU.subtract)
    k.op("scalar", "activation", [ex2], [ex2], ex2, ex2, AF.Ln, bias=np.float32(eps))
    k.op("scalar", "activation", [ex2], [ex2], ex2, ex2, AF.Exp, scale=np.float32(-0.5))
    return mean, ex2


def emit_bcast_rows(k, ctx, row, TM, dst, psname="pC"):
    for t0 in range(0, TM, 512):
        tw = min(512, TM - t0)
        pv = ctx.psum(psname)[0:128, 0:tw]
        rv = View(row.tile, row.p0, row.p1, row.c0 + t0, row.c0 + t0 + tw)
        k.op("tensor", "matmul", [pv], [ctx.ones_row_f[0:1, 0:128], rv],
             pv, ctx.ones_row_f[0:1, 0:128], rv, start=True, stop=True)
        k.op("vector", "tensor_copy", [dst[:, t0:t0 + tw]], [pv],
             dst[:, t0:t0 + tw], pv)


def emit_ln_apply(k, xsrc_chunk, out_chunk, meanB, rstdB, eng1="vector",
                  eng2="gpsimd"):
    k.op(eng1, "tensor_tensor", [out_chunk], [xsrc_chunk, meanB],
         out_chunk, xsrc_chunk, meanB, op=ALU.subtract)
    k.op(eng2, "tensor_tensor", [out_chunk], [out_chunk, rstdB],
         out_chunk, out_chunk, rstdB, op=ALU.mult)


def build_l2ffn_prog(weights, coef=0.5, T_OWN=2048, TM=1024, name="l2ffn",
                     debug=False):
    k = Prog(name)
    ctx = Ctx(k)
    E = 8
    F = 2048
    x_in = k.inp("x", (D, T_OWN))
    y_out = k.out("y", (D, T_OWN))
    if debug:
        dbg_h = k.out("dbg_h", (D, TM))
        dbg_rows = k.out("dbg_rows", (1, 2 * TM))
        dbg_comb = k.out("dbg_comb", (8, TM))
        dbg_score = k.out("dbg_score", (1, TM))
        dbg_shared = k.out("dbg_shared", (D, TM))
        dbg_logit = k.out("dbg_logit", (128, 8))
        dbg_msk = k.out("dbg_msk", (128, 64))
    gate_wT = k.const("gate_wT", weights["gate_wT"])
    sgT = k.const("sgT", weights["sgT"])
    suT = k.const("suT", weights["suT"])
    sdT = k.const("sdT", weights["sdT"])
    rgT = k.const("rgT", weights["rgT"])
    ruT = k.const("ruT", weights["ruT"])
    rdT = k.const("rdT", weights["rdT"])
    idn_f = k.const("idn_f", np.eye(128, dtype=np.float32))
    sel_np = np.zeros((8, 8 * 128), dtype=np.float32)
    for e in range(8):
        sel_np[e, e * 128:(e + 1) * 128] = 1.0
    sel_c = k.const("sel8", sel_np)

    idn = k.sb("idn", 128, 128, F32)
    k.dma("sync", idn.full, (idn_f, idn_f[:, :]))
    sel = k.sb("sel", 8, 8 * 128, BF16)

    xb = FM(k, "xb", D, TM, BF16)
    hbf = FM(k, "hbf", D, TM, BF16)
    xsb = xb
    wg = FM(k, "wg", D, 1024, BF16)
    wu = FM(k, "wu", D, 1024, BF16)
    wd = FM(k, "wd", 1024, 1024, BF16)
    gt = FM(k, "gt", F, TM, BF16)
    ut = FM(k, "ut", 1024, 512, BF16)
    acc = FM(k, "acc", D, TM, F32)
    wstage = [k.sb("wst0", 128, 1024, F32), k.sb("wst1", 128, 1024, F32)]
    xstage = [k.sb("xst0", 128, 512, F32), k.sb("xst1", 128, 512, F32)]
    sq_f = [k.sb("sqf0", 128, 512, F32), k.sb("sqf1", 128, 512, F32)]
    rows = k.sb("rows", 1, 3 * TM, F32)
    aB = k.sb("aB", 128, TM, F32)
    bB = k.sb("bB", 128, TM, F32)
    gwf = FM(k, "gwf", D, 8, F32)
    logits = k.sb("logits", 128, 8, F32)
    logits4 = k.sb("logits4", 128, 8 * (1024 // 128), F32)
    msk = k.sb("msk", 128, 8 * (TM // 128), F32)
    stat = k.sb("stat", 128, 8, F32)
    combT = k.sb("combT", 8, TM, BF16)
    scoreR = k.sb("scoreR", 1, TM, F32)
    mskS = [k.sb("mskS0", 128, 512, BF16), k.sb("mskS1", 128, 512, BF16)]
    scB = aB

    for ci in range(NCHUNK):
        k.dma("sync", gwf.chunk(ci), (gate_wT,
              gate_wT[ci * 128:(ci + 1) * 128, :]))
    k.dma("sync", wstage[1][0:8, 0:1024], (sel_c, sel_c[:, :]))
    k.op("vector", "tensor_copy", [sel.full], [wstage[1][0:8, 0:1024]],
         sel.full, wstage[1][0:8, 0:1024])

    for mb in range(T_OWN // TM):
        tbase = mb * TM
        mean, rstd = emit_ln_stream(k, ctx, x_in, tbase, TM, rows, xstage, sq_f)
        emit_bcast_rows(k, ctx, mean, TM, aB)
        emit_bcast_rows(k, ctx, rstd, TM, bB)
        for t0 in range(0, TM, 512):
            tw = 512
            for ci in range(NCHUNK):
                st = xstage[ci % 2]
                k.dma("sync", st[:, 0:tw],
                      (x_in, x_in[ci * 128:(ci + 1) * 128,
                                  tbase + t0:tbase + t0 + tw]))
                hst = sq_f[ci % 2]
                k.op("vector", "tensor_tensor", [hst[:, 0:tw]],
                     [st[:, 0:tw], aB[:, t0:t0 + tw]],
                     hst[:, 0:tw], st[:, 0:tw], aB[:, t0:t0 + tw],
                     op=ALU.subtract)
                k.op("vector", "tensor_tensor", [hst[:, 0:tw]],
                     [hst[:, 0:tw], bB[:, t0:t0 + tw]],
                     hst[:, 0:tw], hst[:, 0:tw], bB[:, t0:t0 + tw],
                     op=ALU.mult)
                k.op("gpsimd", "tensor_copy", [hbf.chunk(ci, t0, t0 + tw)],
                     [hst[:, 0:tw]], hbf.chunk(ci, t0, t0 + tw), hst[:, 0:tw])
                for tl in range(tw // 128):
                    pGv = ctx.psum(("pA", "pB", "pC", "pD")[tl])[0:128, 0:8]
                    k.op("tensor", "matmul", [pGv],
                         [hst[:, tl * 128:(tl + 1) * 128], gwf.chunk(ci)],
                         pGv, hst[:, tl * 128:(tl + 1) * 128], gwf.chunk(ci),
                         start=(ci == 0), stop=(ci == NCHUNK - 1))
            for tl in range(tw // 128):
                ttg = t0 // 128 + tl
                pGv = ctx.psum(("pA", "pB", "pC", "pD")[tl])[0:128, 0:8]
                k.op("vector", "tensor_copy",
                     [logits4[:, ttg * 8:(ttg + 1) * 8]], [pGv],
                     logits4[:, ttg * 8:(ttg + 1) * 8], pGv)
        ntt = TM // 128
        for tt in range(ntt):
            k.op("vector", "tensor_copy", [logits.full],
                 [logits4[:, tt * 8:(tt + 1) * 8]],
                 logits.full, logits4[:, tt * 8:(tt + 1) * 8])
            k.op("vector", "tensor_reduce", [stat[:, 0:1]], [logits.full],
                 stat[:, 0:1], logits.full, op=ALU.max, axis=AX.X)
            k.op("scalar", "activation", [stat[:, 1:2]], [stat[:, 0:1]],
                 stat[:, 1:2], stat[:, 0:1], AF.Sigmoid)
            mtile = msk[:, tt * 8:(tt + 1) * 8]
            k.op("vector", "tensor_scalar", [mtile], [logits.full, stat[:, 0:1]],
                 mtile, logits.full, stat[:, 0:1], scalar2=None, op0=ALU.is_ge)
            pT = ctx.psum("pC")[0:8, 0:128]
            k.op("tensor", "transpose", [pT], [mtile, idn.full],
                 pT, mtile, idn.full)
            k.op("vector", "tensor_copy",
                 [combT[0:8, tt * 128:(tt + 1) * 128]], [pT],
                 combT[0:8, tt * 128:(tt + 1) * 128], pT)
            pT2 = ctx.psum("pC")[0:1, 0:128]
            k.op("tensor", "transpose", [pT2], [stat[:, 1:2], idn.full],
                 pT2, stat[:, 1:2], idn.full)
            k.op("vector", "tensor_copy",
                 [scoreR[0:1, tt * 128:(tt + 1) * 128]], [pT2],
                 scoreR[0:1, tt * 128:(tt + 1) * 128], pT2)
        if debug and mb == 0:
            k.dma("gpsimd", (dbg_msk, dbg_msk[:, :]), msk.full)
            for ci in range(NCHUNK):
                k.dma("gpsimd", (dbg_h, dbg_h[ci * 128:(ci + 1) * 128, :]),
                      hbf.chunk(ci))
            k.dma("gpsimd", (dbg_rows, dbg_rows[0:1, :]), rows[0:1, 0:2 * TM])
            k.dma("gpsimd", (dbg_comb, dbg_comb[:, :]), combT.full)
            k.dma("gpsimd", (dbg_score, dbg_score[0:1, :]), scoreR[0:1, 0:TM])
        emit_bcast_rows(k, ctx, scoreR[0:1, 0:TM], TM, scB)
        for ci in range(NCHUNK):
            k.op("gpsimd", "tensor_tensor", [xsb.chunk(ci)],
                 [hbf.chunk(ci), scB[:, 0:TM]],
                 xsb.chunk(ci), hbf.chunk(ci), scB[:, 0:TM], op=ALU.mult)

        first_evac = [True] * (8 * (TM // 512))
        for src in range(1 + E):
            shared = (src == 0)
            e = src - 1
            xin = hbf if shared else xsb
            if not shared:
                for tb in range(TM // 512):
                    pv = ctx.psum("pC")[0:128, 0:512]
                    k.op("tensor", "matmul", [pv],
                         [sel[0:8, e * 128:(e + 1) * 128],
                          combT[0:8, tb * 512:(tb + 1) * 512]],
                         pv, sel[0:8, e * 128:(e + 1) * 128],
                         combT[0:8, tb * 512:(tb + 1) * 512],
                         start=True, stop=True)
                    k.op("vector", "tensor_copy", [mskS[tb].full], [pv],
                         mskS[tb].full, pv)
            for fh in range(2):
                wg_d = sgT if shared else rgT
                wu_d = suT if shared else ruT
                r0 = 0 if shared else None
                for ci in range(NCHUNK):
                    st = wstage[ci % 2]
                    if shared:
                        k.dma("sync", st.full,
                              (wg_d, wg_d[ci * 128:(ci + 1) * 128,
                                           fh * 1024:(fh + 1) * 1024]))
                    else:
                        k.dma("sync", st.full,
                              (wg_d, wg_d[e, ci * 128:(ci + 1) * 128,
                                           fh * 1024:(fh + 1) * 1024]))
                    k.op("vector", "tensor_copy", [wg.chunk(ci)], [st.full],
                         wg.chunk(ci), st.full)
                for ci in range(NCHUNK):
                    st = wstage[ci % 2]
                    if shared:
                        k.dma("sync", st.full,
                              (wu_d, wu_d[ci * 128:(ci + 1) * 128,
                                           fh * 1024:(fh + 1) * 1024]))
                    else:
                        k.dma("sync", st.full,
                              (wu_d, wu_d[e, ci * 128:(ci + 1) * 128,
                                           fh * 1024:(fh + 1) * 1024]))
                    k.op("vector", "tensor_copy", [wu.chunk(ci)], [st.full],
                         wu.chunk(ci), st.full)
                for tb in range(TM // 512):
                    t0 = tb * 512
                    def evac_u(pv, co0, cow, tu0, tuw, tb=tb, shared=shared):
                        dst = ut.chunk(co0 // 128, tu0, tu0 + tuw)
                        if shared:
                            k.op("vector", "tensor_copy", [dst], [pv],
                                 dst, pv)
                        else:
                            k.op("vector", "tensor_tensor", [dst],
                                 [pv, mskS[tb][:, tu0:tu0 + tuw]],
                                 dst, pv, mskS[tb][:, tu0:tu0 + tuw],
                                 op=ALU.mult)
                    proj_fm(k, ctx, wu, xin, 1024, 512, evac_u, x_c0=t0)
                    def evac_g(pv, co0, cow, tg0, tgw, fh=fh, t0=t0):
                        dst = gt.chunk(fh * 8 + co0 // 128, t0 + tg0,
                                       t0 + tg0 + tgw)
                        k.op("scalar", "activation", [dst], [pv],
                             dst, pv, AF.Silu)
                    proj_fm(k, ctx, wg, xin, 1024, 512, evac_g, x_c0=t0)
                    for co in range(8):
                        dst = gt.chunk(fh * 8 + co, t0, t0 + 512)
                        k.op("gpsimd", "tensor_tensor", [dst],
                             [dst, ut.chunk(co, 0, 512)],
                             dst, dst, ut.chunk(co, 0, 512), op=ALU.mult)
            wd_d = sdT if shared else rdT
            for dh in range(2):
                for ci in range(8):
                    st = wstage[ci % 2]
                    if shared:
                        k.dma("sync", st.full,
                              (wd_d, wd_d[dh * 1024 + ci * 128:
                                          dh * 1024 + (ci + 1) * 128, :]))
                    else:
                        k.dma("sync", st.full,
                              (wd_d, wd_d[e, dh * 1024 + ci * 128:
                                          dh * 1024 + (ci + 1) * 128, :]))
                    k.op("vector", "tensor_copy", [wd.chunk(ci)], [st.full],
                         wd.chunk(ci), st.full)
                ii = 0
                for co0 in range(0, D, 128):
                    for t0 in range(0, TM, 512):
                        ps = ctx.psum(("pA", "pB")[ii % 2])
                        ii += 1
                        pv = ps[0:128, 0:512]
                        for ci in range(8):
                            k.op("tensor", "matmul", [pv],
                                 [wd.chunk(ci, co0, co0 + 128),
                                  gt.chunk(dh * 8 + ci, t0, t0 + 512)],
                                 pv, wd.chunk(ci, co0, co0 + 128),
                                 gt.chunk(dh * 8 + ci, t0, t0 + 512),
                                 start=(ci == 0), stop=(ci == 7))
                        slot = (co0 // 128) * (TM // 512) + t0 // 512
                        dst = acc.chunk(co0 // 128, t0, t0 + 512)
                        if first_evac[slot]:
                            first_evac[slot] = False
                            k.op("vector", "tensor_copy", [dst], [pv],
                                 dst, pv)
                        else:
                            k.op("vector", "tensor_tensor", [dst], [pv, dst],
                                 dst, pv, dst, op=ALU.add)
        if debug and mb == 0:
            for ci in range(NCHUNK):
                k.dma("gpsimd", (dbg_shared, dbg_shared[ci * 128:(ci + 1) * 128, :]),
                      acc.chunk(ci))
        for ci in range(NCHUNK):
            for hh in range(TM // 512):
                st = xstage[(2 * ci + hh) % 2]
                c0, c1 = hh * 512, (hh + 1) * 512
                k.dma("scalar", st.full,
                      (x_in, x_in[ci * 128:(ci + 1) * 128,
                                  tbase + c0:tbase + c1]))
                k.op("vector", "scalar_tensor_tensor", [acc.chunk(ci, c0, c1)],
                     [acc.chunk(ci, c0, c1), st.full],
                     acc.chunk(ci, c0, c1), acc.chunk(ci, c0, c1), float(coef),
                     st.full, op0=ALU.mult, op1=ALU.add)
                k.dma("gpsimd", (y_out, y_out[ci * 128:(ci + 1) * 128,
                                              tbase + c0:tbase + c1]),
                      acc.chunk(ci, c0, c1))
    k.seal()
    return k


def l2ffn_host_weights(gate_w, shared_gate, shared_up, shared_down,
                       r_gate, r_up, r_down):
    return {
        "gate_wT": np.ascontiguousarray(gate_w.T),
        "sgT": np.ascontiguousarray(shared_gate.T),
        "suT": np.ascontiguousarray(shared_up.T),
        "sdT": np.ascontiguousarray(shared_down.T),
        "rgT": np.ascontiguousarray(np.transpose(r_gate, (0, 2, 1))),
        "ruT": np.ascontiguousarray(np.transpose(r_up, (0, 2, 1))),
        "rdT": np.ascontiguousarray(np.transpose(r_down, (0, 2, 1))),
    }


def emit_ln_stream(k, ctx, x_dram, tbase, TM, rows, xstage, sq_f, xb_out=None,
                   eps=1e-5, nch=NCHUNK, row_off=0, dram_col_off=0,
                   queue="sync", psA="pC", psB="pD", chunk_off=None):
    D_ = nch * 128
    for t0 in range(0, TM, 512):
        tw = min(512, TM - t0)
        pS = ctx.psum(psA)[0:1, 0:tw]
        pQ = ctx.psum(psB)[0:1, 0:tw]
        for ci in range(nch):
            st = xstage[ci % 2]
            coff = dram_col_off + (chunk_off(ci) if chunk_off else 0)
            k.dma(queue, st[:, 0:tw],
                  (x_dram, x_dram[row_off + ci * 128:row_off + (ci + 1) * 128,
                                  coff + tbase + t0:
                                  coff + tbase + t0 + tw]),
                  **({"allow_slow_non_contiguous": True} if tw == 1 else {}))
            k.op("tensor", "matmul", [pS], [ctx.ones_col_f[:, 0:1], st[:, 0:tw]],
                 pS, ctx.ones_col_f[:, 0:1], st[:, 0:tw],
                 start=(ci == 0), stop=(ci == nch - 1))
            sq = sq_f[ci % 2]
            k.op("scalar", "activation", [sq[:, 0:tw]], [st[:, 0:tw]],
                 sq[:, 0:tw], st[:, 0:tw], AF.Square)
            k.op("tensor", "matmul", [pQ], [ctx.ones_col_f[:, 0:1], sq[:, 0:tw]],
                 pQ, ctx.ones_col_f[:, 0:1], sq[:, 0:tw],
                 start=(ci == 0), stop=(ci == nch - 1))
            if xb_out is not None:
                k.op("gpsimd", "tensor_copy", [xb_out.chunk(ci, t0, t0 + tw)],
                     [st[:, 0:tw]], xb_out.chunk(ci, t0, t0 + tw), st[:, 0:tw])
        k.op("vector", "tensor_scalar", [rows[0:1, t0:t0 + tw]], [pS],
             rows[0:1, t0:t0 + tw], pS, 1.0 / D_, scalar2=None, op0=ALU.mult)
        k.op("vector", "tensor_scalar", [rows[0:1, TM + t0:TM + t0 + tw]], [pQ],
             rows[0:1, TM + t0:TM + t0 + tw], pQ, 1.0 / D_, scalar2=None,
             op0=ALU.mult)
    mean = rows[0:1, 0:TM]
    ex2 = rows[0:1, TM:2 * TM]
    tmp = rows[0:1, 2 * TM:3 * TM]
    k.op("vector", "tensor_tensor", [tmp], [mean], tmp, mean, mean, op=ALU.mult)
    k.op("vector", "tensor_tensor", [ex2], [ex2, tmp], ex2, ex2, tmp,
         op=ALU.subtract)
    k.op("scalar", "activation", [ex2], [ex2], ex2, ex2, AF.Ln, bias=np.float32(eps))
    k.op("scalar", "activation", [ex2], [ex2], ex2, ex2, AF.Exp, scale=np.float32(-0.5))
    return mean, ex2


def bf16_np(x):
    import ml_dtypes
    return np.asarray(x, dtype=ml_dtypes.bfloat16)


def hi_lo_bf16(x):
    import ml_dtypes
    hi = np.asarray(x, dtype=ml_dtypes.bfloat16)
    lo = np.asarray(x - hi.astype(np.float32), dtype=ml_dtypes.bfloat16)
    return hi, lo


def emit_ln_full(k, ctx, x_dram, T, hbf, rows, xstage, sq_f, aB, bB, TM=1024,
                 nch=NCHUNK, extra_per_chunk=None):
    for mb in range(T // TM):
        tbase = mb * TM
        mean, rstd = emit_ln_stream(k, ctx, x_dram, tbase, TM, rows, xstage, sq_f)
        emit_bcast_rows(k, ctx, mean, TM, aB)
        emit_bcast_rows(k, ctx, rstd, TM, bB)
        for t0 in range(0, TM, 512):
            tw = 512
            for ci in range(nch):
                st = xstage[ci % 2]
                k.dma("sync", st[:, 0:tw],
                      (x_dram, x_dram[ci * 128:(ci + 1) * 128,
                                      tbase + t0:tbase + t0 + tw]))
                hst = sq_f[ci % 2]
                k.op("vector", "tensor_tensor", [hst[:, 0:tw]],
                     [st[:, 0:tw], aB[:, t0:t0 + tw]],
                     hst[:, 0:tw], st[:, 0:tw], aB[:, t0:t0 + tw],
                     op=ALU.subtract)
                k.op("vector", "tensor_tensor", [hst[:, 0:tw]],
                     [hst[:, 0:tw], bB[:, t0:t0 + tw]],
                     hst[:, 0:tw], hst[:, 0:tw], bB[:, t0:t0 + tw],
                     op=ALU.mult)
                k.op("gpsimd", "tensor_copy",
                     [hbf.chunk(ci, tbase + t0, tbase + t0 + tw)],
                     [hst[:, 0:tw]],
                     hbf.chunk(ci, tbase + t0, tbase + t0 + tw), hst[:, 0:tw])
                if extra_per_chunk is not None:
                    extra_per_chunk(ci, tbase + t0, tw, hst[:, 0:tw])


def build_l0attn_a_prog(consts, T=S, H=8, name="l0attn_a", debug=False,
                        paranoid=False, noexempt=False, serialize_tags=()):
    k = Prog(name)
    ctx = Ctx(k)
    x_in = k.inp("x", (D, T))
    wq_in = k.inp("wq", (D, 64 * H))
    wk_in = k.inp("wk", (D, 64 * H))
    wv_in = k.inp("wv", (D, 64 * H))
    aqhi = k.inp("aqhi", (H, T), BF16)
    aqlo = k.inp("aqlo", (H, T), BF16)
    akhi = k.inp("akhi", (H, T), BF16)
    aklo = k.inp("aklo", (H, T), BF16)
    ones_bf = k.const("ones_bf", consts["ones_bf"])
    cmask = k.const("cmask", consts["cmask"])
    idnb_c = k.const("idnb", bf16_np(np.eye(128)))
    ctxo = k.out("ctxo", (64 * H, T), BF16)
    if debug:
        dbg_qh = k.out("dbg_qh", (70, T), BF16)
        dbg_kh = k.out("dbg_kh", (70, T), BF16)
        dbg_macc = k.out("dbg_macc", (128, T // 128))
        dbg_att = k.out("dbg_att", (128, 512), BF16)
        dbg_den = k.out("dbg_den", (1, 512))
        dbg_h0 = k.out("dbg_h0", (D, 1024), BF16)

    hbf = FM(k, "hbf", D, T, BF16)
    rows = k.sb("rows", 1, 3 * 1024, F32)
    xstage = [k.sb("xst0", 128, 512, F32), k.sb("xst1", 128, 512, F32)]
    sq_f = [k.sb("sqf0", 128, 512, F32), k.sb("sqf1", 128, 512, F32)]
    aB = k.sb("aB", 128, 1024, F32)
    bB = k.sb("bB", 128, 1024, F32)
    wstage = [k.sb("wsg0", 128, 512, F32), k.sb("wsg1", 128, 512, F32)]
    wqb = FM(k, "wqb", D, 64 * H, BF16)
    wkb = FM(k, "wkb", D, 64 * H, BF16)
    wvb = FM(k, "wvb", D, 64 * H, BF16)
    qh = k.sb("qhat", 70, T, BF16)
    kh = k.sb("khat", 70, T, BF16)
    vall = k.sb("vall", 128, (T // 128) * 520, BF16)
    macc = k.sb("macc", 128, T // 128, F32)
    mbf = k.sb("mbf", 128, T // 128, BF16)
    scr = [k.sb("scr0", 128, 512, F32), k.sb("scr1", 128, 512, F32)]
    att = [k.sb(f"att{i}", 128, 512, BF16) for i in range(3)]
    cmk = k.sb("cmk", 128, 4 * 512, BF16)
    recd = k.sb("recd", 1, 512, F32)
    recB = k.sb("recB", 64, 512, F32)
    cseg = [k.sb(f"cseg{i}", 64, 512, BF16) for i in range(2)]
    idnb = k.sb("idnb_s", 128, 128, BF16)
    pT = [k.ps("pT0", 128, 1024, BF16), k.ps("pT1", 128, 1024, BF16)]
    pE = k.ps("pE", 128, 512, F32)
    pF = k.ps("pF", 128, 512, F32)

    k.dma("sync", cmk.full, (cmask, cmask[:, :]))
    k.dma("sync", idnb.full, (idnb_c, idnb_c[:, :]))
    load_w_fm(k, wq_in, D, 64 * H, wstage, wqb)
    load_w_fm(k, wk_in, D, 64 * H, wstage, wkb)
    load_w_fm(k, wv_in, D, 64 * H, wstage, wvb)

    k.mark("ln")
    emit_ln_full(k, ctx, x_in, T, hbf, rows, xstage, sq_f, aB, bB)
    k.mark("vbuild")

    NT = T // 128
    NB = T // 512

    for tt in range(NT):
        ps = ctx.psum(("pA", "pB")[tt % 2])
        pv = ps[0:128, 0:512]
        for ci in range(NCHUNK):
            k.op("tensor", "matmul", [pv],
                 [hbf.chunk(ci, tt * 128, (tt + 1) * 128), wvb.chunk(ci)],
                 pv, hbf.chunk(ci, tt * 128, (tt + 1) * 128), wvb.chunk(ci),
                 start=(ci == 0), stop=(ci == NCHUNK - 1))
        base = tt * 520
        for h in range(H):
            dst = vall[0:128, base + h * 65:base + h * 65 + 64]
            k.op("vector", "tensor_copy", [dst], [pv],
                 dst, ps[0:128, h * 64:h * 64 + 64])
            k.op("gpsimd", "memset",
                 [vall[0:128, base + h * 65 + 64:base + h * 65 + 65]], [],
                 vall[0:128, base + h * 65 + 64:base + h * 65 + 65], 1.0)

    for r in (64, 65, 66):
        k.dma("scalar", kh[r:r + 1, :], (ones_bf, ones_bf[0:1, :]))
    for r in (67, 68):
        k.dma("scalar", qh[r:r + 1, :], (ones_bf, ones_bf[0:1, :]))

    for h in range(H):
        k.mark("kqbuild")
        for seg in range(NB):
            t0 = seg * 512
            psK = ctx.psum(("pC", "pD")[seg % 2])
            pvK = psK[0:64, 0:512]
            for ci in range(NCHUNK):
                k.op("tensor", "matmul", [pvK],
                     [wkb.chunk(ci, h * 64, (h + 1) * 64),
                      hbf.chunk(ci, t0, t0 + 512)],
                     pvK, wkb.chunk(ci, h * 64, (h + 1) * 64),
                     hbf.chunk(ci, t0, t0 + 512),
                     start=(ci == 0), stop=(ci == NCHUNK - 1))
            k.op("vector", "tensor_copy", [kh[0:64, t0:t0 + 512]], [pvK],
                 kh[0:64, t0:t0 + 512], pvK)
        for seg in range(NB):
            t0 = seg * 512
            psQ = ctx.psum(("pC", "pD")[seg % 2])
            pvQ = psQ[0:64, 0:512]
            for ci in range(NCHUNK):
                k.op("tensor", "matmul", [pvQ],
                     [wqb.chunk(ci, h * 64, (h + 1) * 64),
                      hbf.chunk(ci, t0, t0 + 512)],
                     pvQ, wqb.chunk(ci, h * 64, (h + 1) * 64),
                     hbf.chunk(ci, t0, t0 + 512),
                     start=(ci == 0), stop=(ci == NCHUNK - 1))
            k.op("vector", "tensor_scalar", [qh[0:64, t0:t0 + 512]], [pvQ],
                 qh[0:64, t0:t0 + 512], pvQ, 0.125, scalar2=None, op0=ALU.mult)
        k.dma("scalar", kh[67:68, :], (akhi, akhi[h:h + 1, :]))
        k.dma("scalar", kh[68:69, :], (aklo, aklo[h:h + 1, :]))
        k.dma("scalar", qh[65:66, :], (aqhi, aqhi[h:h + 1, :]))
        k.dma("scalar", qh[66:67, :], (aqlo, aqlo[h:h + 1, :]))
        k.op("gpsimd", "memset", [qh[64:65, :]], [], qh[64:65, :], 0.0)

        k.mark("pass1")
        for qt in range(NT):
            nk = qt * 128 + 128
            first = True
            for kt0 in range(0, nk, 512):
                kw = min(512, nk - kt0)
                ps1 = ctx.psum(("pA", "pB")[(kt0 // 512) % 2])
                pv1 = ps1[0:128, 0:kw]
                k.op("tensor", "matmul", [pv1],
                     [qh[0:69, qt * 128:(qt + 1) * 128], kh[0:69, kt0:kt0 + kw]],
                     pv1, qh[0:69, qt * 128:(qt + 1) * 128],
                     kh[0:69, kt0:kt0 + kw], start=True, stop=True)
                k.op("vector", "tensor_scalar",
                     [scr[(kt0 // 512) % 2][:, 0:kw], macc[:, qt:qt + 1]],
                     [pv1] + ([] if first else [macc[:, qt:qt + 1]]),
                     scr[(kt0 // 512) % 2][:, 0:kw], pv1, 1.0,
                     scalar2=(-1e30 if first else macc[:, qt:qt + 1]),
                     op0=ALU.mult, op1=ALU.max,
                     accum_out=macc[:, qt:qt + 1])
                first = False
        k.mark("mrows")
        k.op("vector", "tensor_copy", [mbf.full], [macc.full], mbf.full, macc.full)
        for qt in range(NT):
            pt = pT[qt % 2]
            k.op("tensor", "transpose", [pt[0:1, 0:128]],
                 [mbf[:, qt:qt + 1], idnb.full],
                 pt[0:1, 0:128], mbf[:, qt:qt + 1], idnb.full)
            k.op("vector", "tensor_scalar",
                 [qh[64:65, qt * 128:(qt + 1) * 128]], [pt[0:1, 0:128]],
                 qh[64:65, qt * 128:(qt + 1) * 128], pt[0:1, 0:128], -1.0,
                 scalar2=None, op0=ALU.mult)

        if debug and h == 0:
            k.dma("gpsimd", (dbg_macc, dbg_macc[:, :]), macc.full)
            for r0, rw in ((0, 32), (32, 32), (64, 6)):
                k.dma("gpsimd", (dbg_qh, dbg_qh[r0:r0 + rw, :]), qh[r0:r0 + rw, :])
                k.dma("gpsimd", (dbg_kh, dbg_kh[r0:r0 + rw, :]), kh[r0:r0 + rw, :])
            for ci in range(NCHUNK):
                k.dma("gpsimd", (dbg_h0, dbg_h0[ci * 128:(ci + 1) * 128, :]),
                      hbf.chunk(ci, 0, 1024))
        k.mark("pass2")
        for qb in range(NB):
            q0 = qb * 512
            nkt = qb * 4 + 4
            stages = []
            for kti in range(nkt):
                kt = kti * 128
                ps2 = ctx.psum(("pC", "pD")[kti % 2])
                pv2 = ps2[0:128, 0:512]
                k.op("tensor", "matmul", [pv2],
                     [kh[0:69, kt:kt + 128], qh[0:69, q0:q0 + 512]],
                     pv2, kh[0:69, kt:kt + 128], qh[0:69, q0:q0 + 512],
                     start=True, stop=True)
                stages.append((kti, pv2))
                if len(stages) == 2:
                    _emit_exp_pv(k, ctx, stages.pop(0), h, qb, nkt, att, cmk,
                                 vall, pE)
            _emit_exp_pv(k, ctx, stages.pop(0), h, qb, nkt, att, cmk, vall, pE)
            if debug and h == 0 and qb == 1:
                k.dma("gpsimd", (dbg_att, dbg_att[:, :]), att[(nkt - 1) % 3].full)
            k.op("vector", "reciprocal", [recd[0:1, 0:512]], [pE.full],
                 recd[0:1, 0:512], pE[64:65, 0:512])
            if debug and h == 0 and qb == 1:
                k.dma("gpsimd", (dbg_den, dbg_den[0:1, :]), recd[0:1, 0:512])
            k.op("tensor", "matmul", [pF.full],
                 [ctx.ones_row_f[0:1, 0:64], recd[0:1, 0:512]],
                 pF[0:64, 0:512], ctx.ones_row_f[0:1, 0:64], recd[0:1, 0:512],
                 start=True, stop=True)
            k.op("vector", "tensor_copy", [recB.full], [pF.full],
                 recB[0:64, 0:512], pF[0:64, 0:512])
            cs = cseg[qb % 2]
            k.op("vector", "tensor_tensor", [cs.full], [pE.full, recB.full],
                 cs[0:64, 0:512], pE[0:64, 0:512], recB[0:64, 0:512],
                 op=ALU.mult)
            k.dma("gpsimd", (ctxo, ctxo[h * 64:(h + 1) * 64, q0:q0 + 512]),
                  cs[0:64, 0:512])
    k.seal(paranoid=paranoid, noexempt=noexempt, serialize_tags=serialize_tags)
    return k


def _emit_exp_pv(k, ctx, stage, h, qb, nkt, att, cmk, vall, pE):
    kti, pv2 = stage
    kt = kti * 128
    a = att[kti % 3]
    k.op("scalar", "activation", [a.full], [pv2], a[0:128, 0:512], pv2, AF.Exp)
    p = kti - (nkt - 4)
    if p >= 0:
        k.op("vector", "tensor_tensor", [a.full],
             [a.full, cmk[:, p * 512:(p + 1) * 512]],
             a[0:128, 0:512], a[0:128, 0:512], cmk[:, p * 512:(p + 1) * 512],
             op=ALU.mult)
    vbase = (kt // 128) * 520 + h * 65
    k.op("tensor", "matmul", [pE.full],
         [vall[0:128, vbase:vbase + 65], a.full],
         pE[0:65, 0:512], vall[0:128, vbase:vbase + 65], a[0:128, 0:512],
         start=(kti == 0), stop=(kti == nkt - 1))


def build_attnout_prog(T_OWN=2048, CIN=1024, with_bias=True, coef=1.0,
                       name="attnout"):
    k = Prog(name)
    ctx = Ctx(k)
    x_in = k.inp("x", (D, T_OWN))
    c_in = k.inp("ctxb", (CIN, T_OWN), BF16)
    w_in = k.inp("w", (CIN, D))
    if with_bias:
        b_in = k.inp("b", (D, 1))
    y_out = k.out("y", (D, T_OWN))

    nci = CIN // 128
    cb = FM(k, "cb", CIN, T_OWN, BF16)
    wb = FM(k, "wb", CIN, D, BF16)
    wstage = [k.sb("wsg0", 128, 1024, F32), k.sb("wsg1", 128, 1024, F32)]
    xstage = [k.sb("xst0", 128, 512, F32), k.sb("xst1", 128, 512, F32)]
    bcol = k.sb("bcol", 128, NCHUNK, F32)
    yst = [k.sb("yst0", 128, 512, F32), k.sb("yst1", 128, 512, F32)]

    for ci in range(nci):
        k.dma("sync", cb.chunk(ci), (c_in, c_in[ci * 128:(ci + 1) * 128, :]))
    load_w_fm(k, w_in, CIN, D, wstage, wb)
    if with_bias:
        for co in range(NCHUNK):
            k.dma("sync", bcol[:, co:co + 1], (b_in, b_in[co * 128:(co + 1) * 128, :]))

    ii = 0
    for co in range(NCHUNK):
        for t0 in range(0, T_OWN, 512):
            ps = ctx.psum(("pA", "pB")[ii % 2])
            ii += 1
            pv = ps[0:128, 0:512]
            for ci in range(nci):
                k.op("tensor", "matmul", [pv],
                     [wb.chunk(ci, co * 128, (co + 1) * 128),
                      cb.chunk(ci, t0, t0 + 512)],
                     pv, wb.chunk(ci, co * 128, (co + 1) * 128),
                     cb.chunk(ci, t0, t0 + 512),
                     start=(ci == 0), stop=(ci == nci - 1))
            st = xstage[ii % 2]
            k.dma("scalar", st.full, (x_in, x_in[co * 128:(co + 1) * 128,
                                                 t0:t0 + 512]))
            y = yst[ii % 2]
            if with_bias:
                k.op("vector", "scalar_tensor_tensor", [y.full],
                     [pv, st.full, bcol[:, co:co + 1]],
                     y.full, pv, bcol[:, co:co + 1], st.full,
                     op0=ALU.add, op1=ALU.add)
            else:
                k.op("vector", "scalar_tensor_tensor", [y.full], [pv, st.full],
                     y.full, pv, float(coef), st.full,
                     op0=ALU.mult, op1=ALU.add)
            k.dma("gpsimd", (y_out, y_out[co * 128:(co + 1) * 128, t0:t0 + 512]),
                  y.full)
    k.seal()
    return k


def l0attn_consts(T=S):
    import ml_dtypes
    slopes = [2.0 ** (-(h + 1) / 2.0) for h in range(16)]
    pos = np.arange(T, dtype=np.float64)
    cm = np.zeros((128, 4 * 512), dtype=np.float64)
    for p in range(4):
        qi = np.arange(512)[None, :]
        kj = np.arange(128)[:, None]
        cm[:, p * 512:(p + 1) * 512] = (qi - kj >= p * 128).astype(np.float64)
    out = {
        "ones_bf": bf16_np(np.ones((1, T))),
        "cmask": bf16_np(cm),
    }
    aq = {}
    for half in range(2):
        heads = range(half * 8, half * 8 + 8)
        aqh = np.stack([-slopes[h] / 8.0 * pos for h in heads])
        akh = np.stack([+slopes[h] / 8.0 * pos for h in heads])
        aq[half] = tuple(hi_lo_bf16(v) for v in (aqh, akh))
    out["per_half"] = aq
    return out


def build_l1attn_a_prog(T=S, H=8, name="l1attn_a"):
    k = Prog(name)
    ctx = Ctx(k)
    BLK = 1024
    NBLK = T // BLK
    x_in = k.inp("x", (D, T))
    wq_in = k.inp("wq", (D, 64 * H))
    wk_in = k.inp("wk", (D, 64 * H))
    wv_in = k.inp("wv", (D, 64 * H))
    ones_bf = k.const("ones_bf", bf16_np(np.ones((1, T))))
    idnb_c = k.const("idnb", bf16_np(np.eye(128)))
    ctxo = k.out("ctxo", (64 * H, T), BF16)

    hbf = FM(k, "hbf", D, T, BF16)
    rows = k.sb("rows", 1, 3 * 1024, F32)
    xstage = [k.sb("xst0", 128, 512, F32), k.sb("xst1", 128, 512, F32)]
    sq_f = [k.sb("sqf0", 128, 512, F32), k.sb("sqf1", 128, 512, F32)]
    aB = k.sb("aB", 128, 1024, F32)
    bB = k.sb("bB", 128, 1024, F32)
    wstage = [k.sb("wsg0", 128, 512, F32), k.sb("wsg1", 128, 512, F32)]
    wqb = FM(k, "wqb", D, 64 * H, BF16)
    wkb = FM(k, "wkb", D, 64 * H, BF16)
    wvb = FM(k, "wvb", D, 64 * H, BF16)
    qh = k.sb("qhat", 66, T, BF16)
    kh = k.sb("khat", 66, T, BF16)
    vall = k.sb("vall", 128, (T // 128) * 520, BF16)
    macc = k.sb("macc", 128, (T // 128) * NBLK, F32)
    mhin = k.sb("mhin", 128, (T // 128) * NBLK, BF16)
    mlon = k.sb("mlon", 128, (T // 128) * NBLK, BF16)
    mfh = k.sb("mfh", 128, (T // 128) * NBLK, F32)
    tmp2 = k.sb("tmp2", 128, 2, BF16)
    scr = [k.sb("scr0", 128, 512, F32), k.sb("scr1", 128, 512, F32)]
    att = [k.sb(f"att{i}", 128, 512, BF16) for i in range(3)]
    recd = k.sb("recd", 1, 512, F32)
    recB = k.sb("recB", 64, 512, F32)
    cseg = [k.sb(f"cseg{i}", 64, 512, BF16) for i in range(2)]
    idnb = k.sb("idnb_s", 128, 128, BF16)
    pT = [k.ps("pT0", 128, 1024, BF16), k.ps("pT1", 128, 1024, BF16)]
    pE = k.ps("pE", 128, 512, F32)
    pF = k.ps("pF", 128, 512, F32)

    k.dma("sync", idnb.full, (idnb_c, idnb_c[:, :]))
    load_w_fm(k, wq_in, D, 64 * H, wstage, wqb)
    load_w_fm(k, wk_in, D, 64 * H, wstage, wkb)
    load_w_fm(k, wv_in, D, 64 * H, wstage, wvb)
    emit_ln_full(k, ctx, x_in, T, hbf, rows, xstage, sq_f, aB, bB)

    NT = T // 128
    NB = T // 512

    for tt in range(NT):
        ps = ctx.psum(("pA", "pB")[tt % 2])
        pv = ps[0:128, 0:512]
        for ci in range(NCHUNK):
            k.op("tensor", "matmul", [pv],
                 [hbf.chunk(ci, tt * 128, (tt + 1) * 128), wvb.chunk(ci)],
                 pv, hbf.chunk(ci, tt * 128, (tt + 1) * 128), wvb.chunk(ci),
                 start=(ci == 0), stop=(ci == NCHUNK - 1))
        base = tt * 520
        for h in range(H):
            dst = vall[0:128, base + h * 65:base + h * 65 + 64]
            k.op("vector", "tensor_copy", [dst], [pv],
                 dst, ps[0:128, h * 64:h * 64 + 64])
            k.op("gpsimd", "memset",
                 [vall[0:128, base + h * 65 + 64:base + h * 65 + 65]], [],
                 vall[0:128, base + h * 65 + 64:base + h * 65 + 65], 1.0)

    for r in (64, 65):
        k.dma("scalar", kh[r:r + 1, :], (ones_bf, ones_bf[0:1, :]))

    for h in range(H):
        for seg in range(NB):
            t0 = seg * 512
            psK = ctx.psum(("pC", "pD")[seg % 2])
            pvK = psK[0:64, 0:512]
            for ci in range(NCHUNK):
                k.op("tensor", "matmul", [pvK],
                     [wkb.chunk(ci, h * 64, (h + 1) * 64),
                      hbf.chunk(ci, t0, t0 + 512)],
                     pvK, wkb.chunk(ci, h * 64, (h + 1) * 64),
                     hbf.chunk(ci, t0, t0 + 512),
                     start=(ci == 0), stop=(ci == NCHUNK - 1))
            k.op("vector", "tensor_copy", [kh[0:64, t0:t0 + 512]], [pvK],
                 kh[0:64, t0:t0 + 512], pvK)
        for seg in range(NB):
            t0 = seg * 512
            psQ = ctx.psum(("pC", "pD")[seg % 2])
            pvQ = psQ[0:64, 0:512]
            for ci in range(NCHUNK):
                k.op("tensor", "matmul", [pvQ],
                     [wqb.chunk(ci, h * 64, (h + 1) * 64),
                      hbf.chunk(ci, t0, t0 + 512)],
                     pvQ, wqb.chunk(ci, h * 64, (h + 1) * 64),
                     hbf.chunk(ci, t0, t0 + 512),
                     start=(ci == 0), stop=(ci == NCHUNK - 1))
            k.op("vector", "tensor_scalar", [qh[0:64, t0:t0 + 512]], [pvQ],
                 qh[0:64, t0:t0 + 512], pvQ, 0.125, scalar2=None, op0=ALU.mult)

        for qt in range(NT):
            for b in range(NBLK):
                col = qt * NBLK + b
                first = True
                for kk in range(2):
                    kt0 = b * BLK + kk * 512
                    ps1 = ctx.psum(("pA", "pB")[kk % 2])
                    pv1 = ps1[0:128, 0:512]
                    k.op("tensor", "matmul", [pv1],
                         [qh[0:64, qt * 128:(qt + 1) * 128],
                          kh[0:64, kt0:kt0 + 512]],
                         pv1, qh[0:64, qt * 128:(qt + 1) * 128],
                         kh[0:64, kt0:kt0 + 512], start=True, stop=True)
                    k.op("vector", "tensor_scalar",
                         [scr[kk % 2][:, 0:512], macc[:, col:col + 1]],
                         [pv1] + ([] if first else [macc[:, col:col + 1]]),
                         scr[kk % 2][:, 0:512], pv1, 1.0,
                         scalar2=(-1e30 if first else macc[:, col:col + 1]),
                         op0=ALU.mult, op1=ALU.max,
                         accum_out=macc[:, col:col + 1])
                    first = False
        k.op("vector", "tensor_copy", [mhin.full], [macc.full],
             mhin.full, macc.full)
        k.op("vector", "tensor_copy", [mfh.full], [mhin.full],
             mfh.full, mhin.full)
        k.op("vector", "tensor_tensor", [mfh.full], [macc.full, mfh.full],
             mfh.full, macc.full, mfh.full, op=ALU.subtract)
        k.op("vector", "tensor_copy", [mlon.full], [mfh.full],
             mlon.full, mfh.full)

        for qb in range(NB):
            q0 = qb * 512
            kti_total = 0
            for b in range(NBLK):
                for qt in range(q0 // 128, q0 // 128 + 4):
                    col = qt * NBLK + b
                    k.op("vector", "tensor_scalar", [tmp2[:, 0:1]],
                         [mhin[:, col:col + 1]],
                         tmp2[:, 0:1], mhin[:, col:col + 1], -1.0,
                         scalar2=None, op0=ALU.mult)
                    k.op("vector", "tensor_scalar", [tmp2[:, 1:2]],
                         [mlon[:, col:col + 1]],
                         tmp2[:, 1:2], mlon[:, col:col + 1], -1.0,
                         scalar2=None, op0=ALU.mult)
                    pt = pT[qt % 2]
                    k.op("tensor", "transpose", [pt[0:2, 0:128]],
                         [tmp2[:, 0:2], idnb.full],
                         pt[0:2, 0:128], tmp2[:, 0:2], idnb.full)
                    k.op("vector", "tensor_copy",
                         [qh[64:66, qt * 128:(qt + 1) * 128]],
                         [pt[0:2, 0:128]],
                         qh[64:66, qt * 128:(qt + 1) * 128], pt[0:2, 0:128])
                stages = []
                for kk in range(BLK // 128):
                    kt = b * BLK + kk * 128
                    ps2 = ctx.psum(("pC", "pD")[kk % 2])
                    pv2 = ps2[0:128, 0:512]
                    k.op("tensor", "matmul", [pv2],
                         [kh[0:66, kt:kt + 128], qh[0:66, q0:q0 + 512]],
                         pv2, kh[0:66, kt:kt + 128], qh[0:66, q0:q0 + 512],
                         start=True, stop=True)
                    stages.append((kt, pv2, kti_total))
                    kti_total += 1
                    if len(stages) == 2:
                        _emit_exp_pv_l1(k, stages.pop(0), h, att, vall, pE,
                                        T)
                while stages:
                    _emit_exp_pv_l1(k, stages.pop(0), h, att, vall, pE, T)
            k.op("vector", "reciprocal", [recd[0:1, 0:512]], [pE.full],
                 recd[0:1, 0:512], pE[64:65, 0:512])
            k.op("tensor", "matmul", [pF.full],
                 [ctx.ones_row_f[0:1, 0:64], recd[0:1, 0:512]],
                 pF[0:64, 0:512], ctx.ones_row_f[0:1, 0:64], recd[0:1, 0:512],
                 start=True, stop=True)
            k.op("vector", "tensor_copy", [recB.full], [pF.full],
                 recB[0:64, 0:512], pF[0:64, 0:512])
            cs = cseg[qb % 2]
            k.op("vector", "tensor_tensor", [cs.full], [pE.full, recB.full],
                 cs[0:64, 0:512], pE[0:64, 0:512], recB[0:64, 0:512],
                 op=ALU.mult)
            k.dma("gpsimd", (ctxo, ctxo[h * 64:(h + 1) * 64, q0:q0 + 512]),
                  cs[0:64, 0:512])
    k.seal()
    return k


def _emit_exp_pv_l1(k, stage, h, att, vall, pE, T):
    kt, pv2, kti = stage
    nkt_total = T // 128
    a = att[kti % 3]
    k.op("scalar", "activation", [a.full], [pv2], a[0:128, 0:512], pv2, AF.Exp)
    vbase = (kt // 128) * 520 + h * 65
    k.op("tensor", "matmul", [pE.full],
         [vall[0:128, vbase:vbase + 65], a.full],
         pE[0:65, 0:512], vall[0:128, vbase:vbase + 65], a[0:128, 0:512],
         start=(kti == 0), stop=(kti == nkt_total - 1))


def build_l1ffn_a_prog(T=S, H=8, name="l1ffn_a"):
    k = Prog(name)
    ctx = Ctx(k)
    x_in = k.inp("x", (D, T))
    wq_in = k.inp("wq", (D, 64 * H))
    wlat_in = k.inp("wkva_lat", (D, 128))
    wx0_in = k.inp("wkva_x0", (D, 16))
    wx1_in = k.inp("wkva_x1", (D, 16))
    wkvb_in = k.inp("wkvb", (128, 32 * H))
    wln1_in = k.inp("wln1", (D, 1))
    _co, _si = _mla_tables(T)
    cosq_c = k.const("cosq", bf16_np(_co / 8.0))
    sinq_c = k.const("sinq", bf16_np(_si / 8.0))
    cosk_c = k.const("cosk", bf16_np(_co))
    sink_c = k.const("sink", bf16_np(_si))
    cmask = k.const("cmaskb", _causal_mask_tiles())
    ctxo = k.out("ctxo", (32 * H, T), BF16)

    zr = FM(k, "zr", D, T, BF16)
    rows = k.sb("rows", 1, 5 * 512, F32)
    xstage = [k.sb("xst0", 128, 512, F32), k.sb("xst1", 128, 512, F32)]
    sq_f = [k.sb("sqf0", 128, 512, F32), k.sb("sqf1", 128, 512, F32)]
    aB = k.sb("aB", 128, 512, F32)
    bB = k.sb("bB", 128, 512, F32)
    wcol = k.sb("wcol", 128, NCHUNK, F32)
    wstage = [k.sb("wsg0", 128, 512, F32), k.sb("wsg1", 128, 512, F32)]
    wqb = FM(k, "wqb", D, 64 * H, BF16)
    wlatb = FM(k, "wlatb", D, 128, BF16)
    wx0b = FM(k, "wx0b", D, 16, BF16)
    wx1b = FM(k, "wx1b", D, 16, BF16)
    wkvbb = k.sb("wkvbb", 128, 32 * H, BF16)
    ltb = k.sb("ltb", 128, T, BF16)
    x0t_t = k.sb("x0t", 16, T, BF16)
    x1t_t = k.sb("x1t", 16, T, BF16)
    rsc_t = k.sb("rsc", 16, T, BF16)
    k0t_t = k.sb("k0t", 16, T, BF16)
    cost = k.sb("cost", 16, 2 * T, BF16)
    qh = k.sb("qhat", 66, T, BF16)
    kh = k.sb("khat", 66, T, BF16)
    vall = k.sb("vall", 128, (T // 128) * 8 * 33, BF16)
    macc = k.sb("macc", 128, T // 128, F32)
    mbf = k.sb("mbf", 128, T // 128, BF16)
    scr = [k.sb("scr0", 128, 512, F32), k.sb("scr1", 128, 512, F32)]
    att = [k.sb(f"att{i}", 128, 512, BF16) for i in range(2)]
    cmk = k.sb("cmk", 128, 4 * 512, BF16)
    recd = k.sb("recd", 1, 512, F32)
    recB = k.sb("recB", 32, 512, F32)
    cseg = [k.sb("cseg0", 32, 512, BF16)]
    idnb_c = k.const("idnb", bf16_np(np.eye(128)))
    idnb = k.sb("idnb_s", 128, 128, BF16)
    ones_bf = k.const("ones_bf", bf16_np(np.ones((1, T))))
    pT = [k.ps("pT0", 128, 1024, BF16), k.ps("pT1", 128, 1024, BF16)]
    pE = k.ps("pE", 128, 512, F32)
    pF = k.ps("pF", 128, 512, F32)

    x0t = lambda c0=0, c1=T: x0t_t[0:16, c0:c1]
    x1t = lambda c0=0, c1=T: x1t_t[0:16, c0:c1]
    rsc = lambda c0=0, c1=T: rsc_t[0:16, c0:c1]
    k0t = lambda c0=0, c1=T: k0t_t[0:16, c0:c1]
    k.dma("sync", cmk.full, (cmask, cmask[:, :]))
    k.dma("sync", idnb.full, (idnb_c, idnb_c[:, :]))
    k.dma("sync", cost[0:16, 0:T], (cosk_c, cosk_c[:, :]))
    k.dma("sync", cost[0:16, T:2 * T], (sink_c, sink_c[:, :]))
    load_w_fm(k, wq_in, D, 64 * H, wstage, wqb)
    load_w_fm(k, wlat_in, D, 128, wstage, wlatb)
    load_w_fm(k, wx0_in, D, 16, wstage, wx0b)
    load_w_fm(k, wx1_in, D, 16, wstage, wx1b)
    k.dma("sync", wstage[0][:, 0:32 * H], (wkvb_in, wkvb_in[:, :]))
    k.op("vector", "tensor_copy", [wkvbb.full], [wstage[0][:, 0:32 * H]],
         wkvbb.full, wstage[0][:, 0:32 * H])
    for ci in range(NCHUNK):
        k.dma("sync", wcol[:, ci:ci + 1], (wln1_in,
              wln1_in[ci * 128:(ci + 1) * 128, :]))

    TM = 512
    for mb in range(T // TM):
        tbase = mb * TM
        mean, rstd = emit_ln_stream(k, ctx, x_in, tbase, TM, rows, xstage, sq_f)
        mz = rows[0:1, 3 * TM:4 * TM]
        k.op("vector", "tensor_tensor", [mz], [rstd], mz, rstd, rstd,
             op=ALU.mult)
        k.op("vector", "tensor_scalar", [mz], [mz], mz, mz, -1e-5,
             scalar2=1.0, op0=ALU.mult, op1=ALU.add)
        k.op("scalar", "activation", [mz], [mz], mz, mz, AF.Ln,
             bias=np.float32(1e-5))
        k.op("scalar", "activation", [mz], [mz], mz, mz, AF.Exp,
             scale=np.float32(-0.5))
        aa = rows[0:1, 4 * TM:5 * TM]
        k.op("vector", "tensor_tensor", [aa], [rstd, mz], aa, rstd, mz,
             op=ALU.mult)
        emit_bcast_rows(k, ctx, mean, TM, aB)
        emit_bcast_rows(k, ctx, aa, TM, bB)
        for ci in range(NCHUNK):
            st = xstage[ci % 2]
            k.dma("sync", st[:, 0:512],
                  (x_in, x_in[ci * 128:(ci + 1) * 128, tbase:tbase + 512]))
            hst = sq_f[ci % 2]
            k.op("vector", "tensor_tensor", [hst[:, 0:512]],
                 [st[:, 0:512], aB[:, 0:512]],
                 hst[:, 0:512], st[:, 0:512], aB[:, 0:512],
                 op=ALU.subtract)
            k.op("vector", "tensor_tensor", [hst[:, 0:512]],
                 [hst[:, 0:512], bB[:, 0:512]],
                 hst[:, 0:512], hst[:, 0:512], bB[:, 0:512],
                 op=ALU.mult)
            k.op("gpsimd", "tensor_scalar",
                 [zr.chunk(ci, tbase, tbase + 512)],
                 [hst[:, 0:512], wcol[:, ci:ci + 1]],
                 zr.chunk(ci, tbase, tbase + 512),
                 hst[:, 0:512], wcol[:, ci:ci + 1], scalar2=None,
                 op0=ALU.mult)

    NT = T // 128
    NB = T // 512

    for seg in range(NB):
        t0 = seg * 512
        psL = ctx.psum(("pA", "pB")[seg % 2])
        pvL = psL[0:128, 0:512]
        for ci in range(NCHUNK):
            k.op("tensor", "matmul", [pvL],
                 [wlatb.chunk(ci), zr.chunk(ci, t0, t0 + 512)],
                 pvL, wlatb.chunk(ci), zr.chunk(ci, t0, t0 + 512),
                 start=(ci == 0), stop=(ci == NCHUNK - 1))
        k.op("vector", "tensor_copy", [ltb[:, t0:t0 + 512]], [pvL],
             ltb[:, t0:t0 + 512], pvL)
        for wsrc, dstf in ((wx0b, x0t), (wx1b, x1t)):
            psR = ctx.psum(("pC", "pD")[seg % 2])
            pvR = psR[0:16, 0:512]
            for ci in range(NCHUNK):
                k.op("tensor", "matmul", [pvR],
                     [wsrc.chunk(ci), zr.chunk(ci, t0, t0 + 512)],
                     pvR, wsrc.chunk(ci), zr.chunk(ci, t0, t0 + 512),
                     start=(ci == 0), stop=(ci == NCHUNK - 1))
            k.op("vector", "tensor_copy", [dstf(t0, t0 + 512)], [pvR],
                 dstf(t0, t0 + 512), pvR)
    CK, SK = cost[0:16, 0:T], cost[0:16, T:2 * T]
    k.op("vector", "tensor_tensor", [k0t()], [x0t(), CK],
         k0t(), x0t(), CK, op=ALU.mult)
    k.op("vector", "tensor_tensor", [rsc()], [x1t(), SK],
         rsc(), x1t(), SK, op=ALU.mult)
    k.op("vector", "tensor_tensor", [kh[32:48, :]], [k0t(), rsc()],
         kh[32:48, :], k0t(), rsc(), op=ALU.subtract)
    k.op("vector", "tensor_tensor", [k0t()], [x0t(), SK],
         k0t(), x0t(), SK, op=ALU.mult)
    k.op("vector", "tensor_tensor", [rsc()], [x1t(), CK],
         rsc(), x1t(), CK, op=ALU.mult)
    k.op("vector", "tensor_tensor", [k0t()], [k0t(), rsc()],
         k0t(), k0t(), rsc(), op=ALU.add)
    k.dma("scalar", kh[48:64, :], k0t())
    k.dma("scalar", kh[64:65, :], (ones_bf, ones_bf[0:1, :]))
    CQ, SQ = CK, SK
    k.dma("sync", cost[0:16, 0:T], (cosq_c, cosq_c[:, :]))
    k.dma("sync", cost[0:16, T:2 * T], (sinq_c, sinq_c[:, :]))

    for tt in range(NT):
        for h in range(H):
            psV = ctx.psum(("pA", "pB")[(tt * H + h) % 2])
            pvV = psV[0:128, 0:32]
            k.op("tensor", "matmul", [pvV],
                 [ltb[:, tt * 128:(tt + 1) * 128],
                  wkvbb[:, h * 32:(h + 1) * 32]],
                 pvV, ltb[:, tt * 128:(tt + 1) * 128],
                 wkvbb[:, h * 32:(h + 1) * 32], start=True, stop=True)
            vb = (tt * H + h) * 33
            k.op("vector", "tensor_copy", [vall[0:128, vb:vb + 32]], [pvV],
                 vall[0:128, vb:vb + 32], pvV)
            k.op("gpsimd", "memset", [vall[0:128, vb + 32:vb + 33]], [],
                 vall[0:128, vb + 32:vb + 33], 1.0)

    for h in range(H):
        for seg in range(NB):
            t0 = seg * 512
            psA_ = ctx.psum(("pC", "pD")[seg % 2])
            pvA = psA_[0:48, 0:512]
            for ci in range(NCHUNK):
                k.op("tensor", "matmul", [pvA],
                     [wqb.chunk(ci, h * 64, h * 64 + 48),
                      zr.chunk(ci, t0, t0 + 512)],
                     pvA, wqb.chunk(ci, h * 64, h * 64 + 48),
                     zr.chunk(ci, t0, t0 + 512),
                     start=(ci == 0), stop=(ci == NCHUNK - 1))
            k.op("vector", "tensor_scalar", [qh[0:32, t0:t0 + 512]], [pvA],
                 qh[0:32, t0:t0 + 512], psA_[0:32, 0:512], 0.125,
                 scalar2=None, op0=ALU.mult)
            k.op("vector", "tensor_copy", [x0t(t0, t0 + 512)], [pvA],
                 x0t(t0, t0 + 512), psA_[32:48, 0:512])
            psB_ = ctx.psum(("pC", "pD")[(seg + 1) % 2])
            pvB = psB_[0:16, 0:512]
            for ci in range(NCHUNK):
                k.op("tensor", "matmul", [pvB],
                     [wqb.chunk(ci, h * 64 + 48, h * 64 + 64),
                      zr.chunk(ci, t0, t0 + 512)],
                     pvB, wqb.chunk(ci, h * 64 + 48, h * 64 + 64),
                     zr.chunk(ci, t0, t0 + 512),
                     start=(ci == 0), stop=(ci == NCHUNK - 1))
            k.op("vector", "tensor_copy", [x1t(t0, t0 + 512)], [pvB],
                 x1t(t0, t0 + 512), pvB)
        k.op("vector", "tensor_tensor", [k0t()], [x0t(), CQ],
             k0t(), x0t(), CQ, op=ALU.mult)
        k.op("vector", "tensor_tensor", [rsc()], [x1t(), SQ],
             rsc(), x1t(), SQ, op=ALU.mult)
        k.op("vector", "tensor_tensor", [qh[32:48, :]], [k0t(), rsc()],
             qh[32:48, :], k0t(), rsc(), op=ALU.subtract)
        k.op("vector", "tensor_tensor", [k0t()], [x0t(), SQ],
             k0t(), x0t(), SQ, op=ALU.mult)
        k.op("vector", "tensor_tensor", [rsc()], [x1t(), CQ],
             rsc(), x1t(), CQ, op=ALU.mult)
        k.op("vector", "tensor_tensor", [k0t()], [k0t(), rsc()],
             k0t(), k0t(), rsc(), op=ALU.add)
        k.dma("scalar", qh[48:64, :], k0t())
        for seg in range(NB):
            t0 = seg * 512
            psK = ctx.psum(("pC", "pD")[seg % 2])
            pvK = psK[0:32, 0:512]
            k.op("tensor", "matmul", [pvK],
                 [wkvbb[:, h * 32:(h + 1) * 32], ltb[:, t0:t0 + 512]],
                 pvK, wkvbb[:, h * 32:(h + 1) * 32], ltb[:, t0:t0 + 512],
                 start=True, stop=True)
            k.op("vector", "tensor_copy", [kh[0:32, t0:t0 + 512]], [pvK],
                 kh[0:32, t0:t0 + 512], pvK)
        k.op("gpsimd", "memset", [qh[64:65, :]], [], qh[64:65, :], 0.0)

        for qt in range(NT):
            nk = qt * 128 + 128
            first = True
            for kt0 in range(0, nk, 512):
                kw = min(512, nk - kt0)
                ps1 = ctx.psum(("pA", "pB")[(kt0 // 512) % 2])
                pv1 = ps1[0:128, 0:kw]
                k.op("tensor", "matmul", [pv1],
                     [qh[0:65, qt * 128:(qt + 1) * 128], kh[0:65, kt0:kt0 + kw]],
                     pv1, qh[0:65, qt * 128:(qt + 1) * 128],
                     kh[0:65, kt0:kt0 + kw], start=True, stop=True)
                k.op("vector", "tensor_scalar",
                     [scr[(kt0 // 512) % 2][:, 0:kw], macc[:, qt:qt + 1]],
                     [pv1] + ([] if first else [macc[:, qt:qt + 1]]),
                     scr[(kt0 // 512) % 2][:, 0:kw], pv1, 1.0,
                     scalar2=(-1e30 if first else macc[:, qt:qt + 1]),
                     op0=ALU.mult, op1=ALU.max,
                     accum_out=macc[:, qt:qt + 1])
                first = False
        k.op("vector", "tensor_copy", [mbf.full], [macc.full], mbf.full,
             macc.full)
        for qt in range(NT):
            pt = pT[qt % 2]
            k.op("tensor", "transpose", [pt[0:1, 0:128]],
                 [mbf[:, qt:qt + 1], idnb.full],
                 pt[0:1, 0:128], mbf[:, qt:qt + 1], idnb.full)
            k.op("vector", "tensor_scalar",
                 [qh[64:65, qt * 128:(qt + 1) * 128]], [pt[0:1, 0:128]],
                 qh[64:65, qt * 128:(qt + 1) * 128], pt[0:1, 0:128], -1.0,
                 scalar2=None, op0=ALU.mult)

        for qb in range(NB):
            q0 = qb * 512
            nkt = qb * 4 + 4
            stages = []
            for kti in range(nkt):
                kt = kti * 128
                ps2 = ctx.psum(("pC", "pD")[kti % 2])
                pv2 = ps2[0:128, 0:512]
                k.op("tensor", "matmul", [pv2],
                     [kh[0:65, kt:kt + 128], qh[0:65, q0:q0 + 512]],
                     pv2, kh[0:65, kt:kt + 128], qh[0:65, q0:q0 + 512],
                     start=True, stop=True)
                stages.append((kti, pv2))
                if len(stages) == 2:
                    _emit_exp_pv_mla(k, stages.pop(0), h, qb, nkt, att, cmk,
                                     vall, pE, H)
            _emit_exp_pv_mla(k, stages.pop(0), h, qb, nkt, att, cmk, vall,
                             pE, H)
            k.op("vector", "reciprocal", [recd[0:1, 0:512]], [pE.full],
                 recd[0:1, 0:512], pE[32:33, 0:512])
            k.op("tensor", "matmul", [pF.full],
                 [ctx.ones_row_f[0:1, 0:32], recd[0:1, 0:512]],
                 pF[0:32, 0:512], ctx.ones_row_f[0:1, 0:32], recd[0:1, 0:512],
                 start=True, stop=True)
            k.op("vector", "tensor_copy", [recB.full], [pF.full],
                 recB[0:32, 0:512], pF[0:32, 0:512])
            cs = cseg[0]
            k.op("vector", "tensor_tensor", [cs.full], [pE.full, recB.full],
                 cs[0:32, 0:512], pE[0:32, 0:512], recB[0:32, 0:512],
                 op=ALU.mult)
            k.dma("gpsimd", (ctxo, ctxo[h * 32:(h + 1) * 32, q0:q0 + 512]),
                  cs[0:32, 0:512])
    k.seal()
    return k


def _emit_exp_pv_mla(k, stage, h, qb, nkt, att, cmk, vall, pE, H):
    kti, pv2 = stage
    kt = kti * 128
    a = att[kti % len(att)]
    k.op("scalar", "activation", [a.full], [pv2], a[0:128, 0:512], pv2, AF.Exp)
    p = kti - (nkt - 4)
    if p >= 0:
        k.op("vector", "tensor_tensor", [a.full],
             [a.full, cmk[:, p * 512:(p + 1) * 512]],
             a[0:128, 0:512], a[0:128, 0:512], cmk[:, p * 512:(p + 1) * 512],
             op=ALU.mult)
    vb = ((kt // 128) * H + h) * 33
    k.op("tensor", "matmul", [pE.full],
         [vall[0:128, vb:vb + 33], a.full],
         pE[0:33, 0:512], vall[0:128, vb:vb + 33], a[0:128, 0:512],
         start=(kti == 0), stop=(kti == nkt - 1))



def _mla_tables(T):
    d = 32
    t = np.arange(T, dtype=np.float64)
    freqs = 1.0 / (100000.0 ** (np.arange(0, d, 2, dtype=np.float64) / d))
    ang = np.outer(freqs, t)
    return np.cos(ang), np.sin(ang)


def _causal_mask_tiles():
    cm = np.zeros((128, 4 * 512), dtype=np.float64)
    for p in range(4):
        qi = np.arange(512)[None, :]
        kj = np.arange(128)[:, None]
        cm[:, p * 512:(p + 1) * 512] = (qi - kj >= p * 128).astype(np.float64)
    return bf16_np(cm)


def build_l1ffn_b_prog(T_OWN=2048, TM=1024, coef=0.5, name="l1ffn_b"):
    k = Prog(name)
    ctx = Ctx(k)
    E, I_ = 8, 512
    x_in = k.inp("x", (D, T_OWN))
    mctx_in = k.inp("mctx", (512, T_OWN), BF16)
    wo_in = k.inp("wo_nope", (512, D))
    wln2_in = k.inp("wln2", (D, 1))
    wgate_in = k.inp("wgate", (D, 8))
    ebias_in = k.inp("ebias", (1, 8))
    wg_in = k.inp("wg", (E, D, I_))
    wu_in = k.inp("wu", (E, D, I_))
    wd_in = k.inp("wd", (E, I_, D))
    swg_in = k.inp("swg", (D, I_))
    swu_in = k.inp("swu", (D, I_))
    swd_in = k.inp("swd", (I_, D))
    idn_f = k.const("idn_f", np.eye(128, dtype=np.float32))
    sel_np = np.zeros((8, 8 * 128), dtype=np.float32)
    for e in range(8):
        sel_np[e, e * 128:(e + 1) * 128] = 1.0
    sel_c = k.const("sel8", sel_np)
    y_out = k.out("y", (D, T_OWN))

    z1 = FM(k, "z1", D, TM, F32)
    r2 = FM(k, "r2", D, TM, BF16)
    acc = FM(k, "acc", D, TM, F32)
    mcb = FM(k, "mcb", 512, TM, BF16)
    wob = FM(k, "wob", 512, D, BF16)
    wgb = FM(k, "wgb", D, I_, BF16)
    wub = FM(k, "wub", D, I_, BF16)
    wdb = FM(k, "wdb", I_, D, BF16)
    gt = FM(k, "gt", I_, TM, BF16)
    ut = FM(k, "ut", I_, 512, BF16)
    gwf = FM(k, "gwf", D, 8, F32)
    rows = k.sb("rows", 1, 3 * TM, F32)
    xstage = [k.sb("xst0", 128, 512, F32), k.sb("xst1", 128, 512, F32)]
    sq_f = [k.sb("sqf0", 128, 512, F32), k.sb("sqf1", 128, 512, F32)]
    aB = k.sb("aB", 128, TM, F32)
    bB = k.sb("bB", 128, TM, F32)
    wstage = [k.sb("wsg0", 128, 1024, F32), k.sb("wsg1", 128, 1024, F32)]
    wcol = k.sb("wcol", 128, NCHUNK, F32)
    logits = k.sb("logits", 128, 8, F32)
    logits4 = k.sb("logits4", 128, 8 * (TM // 128), F32)
    ebB = k.sb("ebB", 128, 8, F32)
    srt = k.sb("srt", 128, 8, F32)
    stat = k.sb("stat", 128, 8, F32)
    cmb = k.sb("cmb", 128, 8, F32)
    msk8 = k.sb("msk8", 128, 8, F32)
    combT = k.sb("combT", 8, TM, F32)
    mskS = [k.sb("mskS0", 128, 512, F32), k.sb("mskS1", 128, 512, F32)]
    idn = k.sb("idn", 128, 128, F32)
    sel = k.sb("sel", 8, 8 * 128, F32)

    k.dma("sync", idn.full, (idn_f, idn_f[:, :]))
    load_w_fm(k, wo_in, 512, D, wstage, wob)
    for ci in range(NCHUNK):
        k.dma("sync", gwf.chunk(ci), (wgate_in, wgate_in[ci * 128:(ci + 1) * 128, :]))
        k.dma("sync", wcol[:, ci:ci + 1], (wln2_in, wln2_in[ci * 128:(ci + 1) * 128, :]))
    k.dma("sync", sel.full, (sel_c, sel_c[:, :]))
    k.dma("sync", wstage[0][0:1, 0:8], (ebias_in, ebias_in[0:1, :]))
    pEB = ctx.psum("pC")[0:128, 0:8]
    k.op("tensor", "matmul", [pEB],
         [ctx.ones_row_f[0:1, 0:128], wstage[0][0:1, 0:8]],
         pEB, ctx.ones_row_f[0:1, 0:128], wstage[0][0:1, 0:8],
         start=True, stop=True)
    k.op("vector", "tensor_copy", [ebB.full], [pEB], ebB.full, pEB)

    for mb in range(T_OWN // TM):
        tbase = mb * TM
        for ci in range(4):
            k.dma("sync", mcb.chunk(ci),
                  (mctx_in, mctx_in[ci * 128:(ci + 1) * 128, tbase:tbase + TM]))
        mean, rstd = emit_ln_stream(k, ctx, x_in, tbase, TM, rows, xstage, sq_f)
        emit_bcast_rows(k, ctx, mean, TM, aB)
        emit_bcast_rows(k, ctx, rstd, TM, bB)
        ii = 0
        for co in range(NCHUNK):
            for t0 in range(0, TM, 512):
                ps = ctx.psum(("pA", "pB")[ii % 2])
                ii += 1
                pv = ps[0:128, 0:512]
                for ci in range(4):
                    k.op("tensor", "matmul", [pv],
                         [wob.chunk(ci, co * 128, (co + 1) * 128),
                          mcb.chunk(ci, t0, t0 + 512)],
                         pv, wob.chunk(ci, co * 128, (co + 1) * 128),
                         mcb.chunk(ci, t0, t0 + 512),
                         start=(ci == 0), stop=(ci == 3))
                st = xstage[ii % 2]
                k.dma("scalar", st[:, 0:512],
                      (x_in, x_in[co * 128:(co + 1) * 128,
                                  tbase + t0:tbase + t0 + 512]))
                hst = sq_f[ii % 2]
                k.op("vector", "tensor_tensor", [hst[:, 0:512]],
                     [st[:, 0:512], aB[:, t0:t0 + 512]],
                     hst[:, 0:512], st[:, 0:512], aB[:, t0:t0 + 512],
                     op=ALU.subtract)
                k.op("vector", "tensor_tensor", [hst[:, 0:512]],
                     [hst[:, 0:512], bB[:, t0:t0 + 512]],
                     hst[:, 0:512], hst[:, 0:512], bB[:, t0:t0 + 512],
                     op=ALU.mult)
                k.op("vector", "tensor_tensor",
                     [z1.chunk(co, t0, t0 + 512)], [hst[:, 0:512], pv],
                     z1.chunk(co, t0, t0 + 512), hst[:, 0:512], pv,
                     op=ALU.add)
        for t0 in range(0, TM, 512):
            pQ = ctx.psum("pD")[0:1, 0:512]
            for ci in range(NCHUNK):
                sq = sq_f[ci % 2]
                k.op("scalar", "activation", [sq[:, 0:512]],
                     [z1.chunk(ci, t0, t0 + 512)],
                     sq[:, 0:512], z1.chunk(ci, t0, t0 + 512), AF.Square)
                k.op("tensor", "matmul", [pQ],
                     [ctx.ones_col_f[:, 0:1], sq[:, 0:512]],
                     pQ, ctx.ones_col_f[:, 0:1], sq[:, 0:512],
                     start=(ci == 0), stop=(ci == NCHUNK - 1))
            k.op("vector", "tensor_scalar", [rows[0:1, t0:t0 + 512]], [pQ],
                 rows[0:1, t0:t0 + 512], pQ, 1.0 / D, scalar2=None,
                 op0=ALU.mult)
        rr = rows[0:1, 0:TM]
        k.op("scalar", "activation", [rr], [rr], rr, rr, AF.Ln,
             bias=np.float32(1e-5))
        k.op("scalar", "activation", [rr], [rr], rr, rr, AF.Exp,
             scale=np.float32(-0.5))
        emit_bcast_rows(k, ctx, rr, TM, bB)
        for t0 in range(0, TM, 512):
            for ci in range(NCHUNK):
                hst = sq_f[ci % 2]
                k.op("vector", "tensor_tensor", [hst[:, 0:512]],
                     [z1.chunk(ci, t0, t0 + 512), bB[:, t0:t0 + 512]],
                     hst[:, 0:512], z1.chunk(ci, t0, t0 + 512),
                     bB[:, t0:t0 + 512], op=ALU.mult)
                k.op("vector", "tensor_scalar", [hst[:, 0:512]],
                     [hst[:, 0:512], wcol[:, ci:ci + 1]],
                     hst[:, 0:512], hst[:, 0:512], wcol[:, ci:ci + 1],
                     scalar2=None, op0=ALU.mult)
                k.op("gpsimd", "tensor_copy",
                     [r2.chunk(ci, t0, t0 + 512)], [hst[:, 0:512]],
                     r2.chunk(ci, t0, t0 + 512), hst[:, 0:512])
                for tl in range(4):
                    pGv = ctx.psum(("pA", "pB", "pC", "pD")[tl])[0:128, 0:8]
                    k.op("tensor", "matmul", [pGv],
                         [hst[:, tl * 128:(tl + 1) * 128], gwf.chunk(ci)],
                         pGv, hst[:, tl * 128:(tl + 1) * 128], gwf.chunk(ci),
                         start=(ci == 0), stop=(ci == NCHUNK - 1))
            for tl in range(4):
                ttg = t0 // 128 + tl
                pGv = ctx.psum(("pA", "pB", "pC", "pD")[tl])[0:128, 0:8]
                k.op("vector", "tensor_copy",
                     [logits4[:, ttg * 8:(ttg + 1) * 8]], [pGv],
                     logits4[:, ttg * 8:(ttg + 1) * 8], pGv)
        for tt in range(TM // 128):
            k.op("vector", "tensor_tensor", [logits.full],
                 [logits4[:, tt * 8:(tt + 1) * 8], ebB.full],
                 logits.full, logits4[:, tt * 8:(tt + 1) * 8], ebB.full,
                 op=ALU.add)
            k.op("vector", "max", [srt.full], [logits.full],
                 srt.full, logits.full)
            k.op("vector", "tensor_scalar", [stat[:, 0:2]],
                 [srt.full], stat[:, 0:2], srt[:, 0:2], srt[:, 0:1],
                 scalar2=None, op0=ALU.subtract)
            k.op("scalar", "activation", [stat[:, 2:4]], [stat[:, 0:2]],
                 stat[:, 2:4], stat[:, 0:2], AF.Exp)
            k.op("vector", "tensor_reduce", [stat[:, 4:5]], [stat[:, 2:4]],
                 stat[:, 4:5], stat[:, 2:4], op=ALU.add, axis=AX.X)
            k.op("vector", "reciprocal", [stat[:, 5:6]], [stat[:, 4:5]],
                 stat[:, 5:6], stat[:, 4:5])
            k.op("vector", "tensor_scalar", [cmb.full],
                 [logits.full, srt[:, 0:1]],
                 cmb.full, logits.full, srt[:, 0:1], scalar2=None,
                 op0=ALU.subtract)
            k.op("scalar", "activation", [cmb.full], [cmb.full],
                 cmb.full, cmb.full, AF.Exp)
            k.op("vector", "tensor_scalar", [msk8.full],
                 [logits.full, srt[:, 1:2]],
                 msk8.full, logits.full, srt[:, 1:2], scalar2=None,
                 op0=ALU.is_ge)
            k.op("vector", "tensor_tensor", [cmb.full], [cmb.full, msk8.full],
                 cmb.full, cmb.full, msk8.full, op=ALU.mult)
            k.op("vector", "tensor_scalar", [cmb.full],
                 [cmb.full, stat[:, 5:6]],
                 cmb.full, cmb.full, stat[:, 5:6], scalar2=None, op0=ALU.mult)
            pT_ = ctx.psum("pC")[0:8, 0:128]
            k.op("tensor", "transpose", [pT_], [cmb.full, idn.full],
                 pT_, cmb.full, idn.full)
            k.op("vector", "tensor_copy",
                 [combT[0:8, tt * 128:(tt + 1) * 128]], [pT_],
                 combT[0:8, tt * 128:(tt + 1) * 128], pT_)

        first_evac = [True] * (NCHUNK * (TM // 512))
        for src in range(1 + E):
            shared = (src == 0)
            e = src - 1
            for ci in range(NCHUNK):
                st = wstage[ci % 2]
                if shared:
                    k.dma("sync", st[:, 0:I_], (swg_in, swg_in[ci * 128:(ci + 1) * 128, :]))
                else:
                    k.dma("sync", st[:, 0:I_], (wg_in, wg_in[e, ci * 128:(ci + 1) * 128, :]))
                k.op("vector", "tensor_copy", [wgb.chunk(ci)], [st[:, 0:I_]],
                     wgb.chunk(ci), st[:, 0:I_])
            for ci in range(NCHUNK):
                st = wstage[ci % 2]
                if shared:
                    k.dma("sync", st[:, 0:I_], (swu_in, swu_in[ci * 128:(ci + 1) * 128, :]))
                else:
                    k.dma("sync", st[:, 0:I_], (wu_in, wu_in[e, ci * 128:(ci + 1) * 128, :]))
                k.op("vector", "tensor_copy", [wub.chunk(ci)], [st[:, 0:I_]],
                     wub.chunk(ci), st[:, 0:I_])
            for ci in range(4):
                st = wstage[ci % 2]
                if shared:
                    k.dma("sync", st[:, 0:1024], (swd_in, swd_in[ci * 128:(ci + 1) * 128, :]))
                else:
                    k.dma("sync", st[:, 0:1024], (wd_in, wd_in[e, ci * 128:(ci + 1) * 128, :]))
                k.op("vector", "tensor_copy", [wdb.chunk(ci)], [st[:, 0:1024]],
                     wdb.chunk(ci), st[:, 0:1024])
            if not shared:
                for tb in range(TM // 512):
                    pv = ctx.psum("pC")[0:128, 0:512]
                    k.op("tensor", "matmul", [pv],
                         [sel[0:8, e * 128:(e + 1) * 128],
                          combT[0:8, tb * 512:(tb + 1) * 512]],
                         pv, sel[0:8, e * 128:(e + 1) * 128],
                         combT[0:8, tb * 512:(tb + 1) * 512],
                         start=True, stop=True)
                    k.op("vector", "tensor_copy", [mskS[tb].full], [pv],
                         mskS[tb].full, pv)
            for tb in range(TM // 512):
                t0 = tb * 512

                def evac_u(pv, co0, cow, tu0, tuw, tb=tb, shared=shared):
                    dst = ut.chunk(co0 // 128, tu0, tu0 + tuw)
                    if shared:
                        k.op("vector", "tensor_copy", [dst], [pv], dst, pv)
                    else:
                        k.op("vector", "tensor_tensor", [dst],
                             [pv, mskS[tb][:, tu0:tu0 + tuw]],
                             dst, pv, mskS[tb][:, tu0:tu0 + tuw],
                             op=ALU.mult)
                proj_fm(k, ctx, wub, r2, I_, 512, evac_u, x_c0=t0)

                def evac_g(pv, co0, cow, tg0, tgw, t0=t0):
                    dst = gt.chunk(co0 // 128, t0 + tg0, t0 + tg0 + tgw)
                    k.op("scalar", "activation", [dst], [pv], dst, pv, AF.Silu)
                proj_fm(k, ctx, wgb, r2, I_, 512, evac_g, x_c0=t0)
                for co in range(I_ // 128):
                    dst = gt.chunk(co, t0, t0 + 512)
                    k.op("gpsimd", "tensor_tensor", [dst],
                         [dst, ut.chunk(co, 0, 512)],
                         dst, dst, ut.chunk(co, 0, 512), op=ALU.mult)
            ii = 0
            for co0 in range(0, D, 128):
                for t0 in range(0, TM, 512):
                    ps = ctx.psum(("pA", "pB")[ii % 2])
                    ii += 1
                    pv = ps[0:128, 0:512]
                    for ci in range(4):
                        k.op("tensor", "matmul", [pv],
                             [wdb.chunk(ci, co0, co0 + 128),
                              gt.chunk(ci, t0, t0 + 512)],
                             pv, wdb.chunk(ci, co0, co0 + 128),
                             gt.chunk(ci, t0, t0 + 512),
                             start=(ci == 0), stop=(ci == 3))
                    slot = (co0 // 128) * (TM // 512) + t0 // 512
                    dst = acc.chunk(co0 // 128, t0, t0 + 512)
                    if first_evac[slot]:
                        first_evac[slot] = False
                        k.op("vector", "tensor_copy", [dst], [pv], dst, pv)
                    else:
                        k.op("vector", "tensor_tensor", [dst], [pv, dst],
                             dst, pv, dst, op=ALU.add)
        for ci in range(NCHUNK):
            for hh in range(TM // 512):
                st = xstage[(2 * ci + hh) % 2]
                c0, c1 = hh * 512, (hh + 1) * 512
                k.dma("scalar", st[:, 0:512],
                      (x_in, x_in[ci * 128:(ci + 1) * 128, tbase + c0:tbase + c1]))
                k.op("vector", "tensor_tensor", [acc.chunk(ci, c0, c1)],
                     [acc.chunk(ci, c0, c1), z1.chunk(ci, c0, c1)],
                     acc.chunk(ci, c0, c1), acc.chunk(ci, c0, c1),
                     z1.chunk(ci, c0, c1), op=ALU.add)
                k.op("vector", "scalar_tensor_tensor", [acc.chunk(ci, c0, c1)],
                     [acc.chunk(ci, c0, c1), st[:, 0:512]],
                     acc.chunk(ci, c0, c1), acc.chunk(ci, c0, c1), float(coef),
                     st[:, 0:512], op0=ALU.mult, op1=ALU.add)
                k.dma("gpsimd", (y_out, y_out[ci * 128:(ci + 1) * 128,
                                              tbase + c0:tbase + c1]),
                      acc.chunk(ci, c0, c1))
    k.seal()
    return k


def build_lwinA_prog(T_in, NQW, OFF, with_ln, FMW, name="lwinA"):
    H = 8
    k = Prog(name)
    ctx = Ctx(k)
    TQ = NQW * 512
    QS = OFF + 512
    TK = 512 * (NQW + 1)
    x_in = k.inp("x", (D, T_in))
    wqkv_in = k.inp("wqkv", (D, 1536))
    qscol_in = k.inp("qscol", (128, 1))
    kscol_in = k.inp("kscol", (128, 1))
    tab_in = {n: k.inp(n, (128, T_in), BF16) for n in ("cq", "sqsw", "ck", "sksw")}
    pmask4_in = k.inp("pmask4", (128, 4 * 512), BF16)
    wo_in = k.inp("wo", (512, D))
    x1_out = k.out("x1", (D, TQ))
    cmask_c = k.const("cmaskw", _causal_mask_tiles())
    pm = np.zeros((128, 4 * 512))
    for p in range(4):
        qi = np.arange(512)[None, :]
        kj = np.arange(128)[:, None]
        pm[:, p * 512:(p + 1) * 512] = (qi <= kj + p * 128)
    pmask_c = k.const("pmaskw", bf16_np(pm))
    idnb_c = k.const("idnbw", bf16_np(np.eye(128)))
    onesb_c = k.const("onesbw", bf16_np(np.ones((1, T_in))))
    o2c_np = np.zeros((128, 2), dtype=np.float32)
    o2c_np[0:64, 0] = 1.0
    o2c_np[64:128, 1] = 1.0
    ones2c_c = k.const("ones2c", bf16_np(o2c_np))
    ones2r_c = k.const("ones2r", np.ascontiguousarray(o2c_np.T))

    zd = k.dram_scratch("zd", (D, T_in))

    rows = k.sb("rows", 1, 3 * 512, F32)
    xstage = [k.sb("xst0", 128, 513, F32), k.sb("xst1", 128, 513, F32)]
    sq_f = [k.sb("sqf0", 128, 512, F32), k.sb("sqf1", 128, 512, F32)]
    aB = k.sb("aB", 128, 512, F32)
    bB = k.sb("bB", 128, 512, F32)
    wstage = [k.sb("wsg0", 128, 1024, F32), k.sb("wsg1", 128, 1024, F32)]
    wqkvb = FM(k, "wqkvb", D, 1536, BF16)
    wob = FM(k, "wob", 512, D, BF16)
    tabs = k.sb("tabs", 128, 2 * 512, BF16)
    qsc = k.sb("qsc", 128, 2, F32)
    ones2 = k.sb("ones2", 128, 2, BF16)
    ones2r = k.sb("ones2r_s", 2, 128, F32)
    ssr = k.sb("ssr", 2, 512, F32)
    krs = k.sb("krs", 128, 512, BF16)
    wsc = k.sb("wsc", 128, 512, BF16)
    wsh = k.sb("wsh", 128, 512, BF16)
    kbn = k.sb("kbn", 128, 512, BF16)
    k2res = k.sb("k2res", 128, T_in, BF16)
    q2res = k.sb("q2res", 128, TQ, BF16)
    kh = k.sb("khat", 65, T_in, BF16)
    qh = k.sb("qhat", 65, TQ, BF16)
    vall = k.sb("vall", 128, (TK // 128) * 520, BF16)
    oall = FM(k, "oall", 512, TQ, BF16)
    macc = k.sb("maccw", 128, TQ // 128, F32)
    mbf = k.sb("mbfw", 128, TQ // 128, BF16)
    scr = [k.sb("scr0", 128, 512, F32), k.sb("scr1", 128, 512, F32)]
    att = [k.sb(f"att{i}", 128, 512, BF16) for i in range(2)]
    cmk = k.sb("cmkw", 128, 4 * 512, BF16)
    pmk = k.sb("pmkw", 128, 4 * 512, BF16)
    pmk4 = k.sb("pmk4w", 128, 4 * 512, BF16)
    idnb = k.sb("idnbw_s", 128, 128, BF16)
    recd = k.sb("recd", 1, 512, F32)
    recB = k.sb("recB", 64, 512, F32)
    cseg = k.sb("cseg0", 64, 512, BF16)
    pT = [k.ps("pT0", 128, 1024, BF16), k.ps("pT1", 128, 1024, BF16)]
    pE = ctx.psum("pE")
    pF = ctx.psum("pF")

    k.dma("sync", cmk.full, (cmask_c, cmask_c[:, :]))
    k.dma("sync", pmk.full, (pmask_c, pmask_c[:, :]))
    k.dma("sync", pmk4.full, (pmask4_in, pmask4_in[:, :]))
    k.dma("sync", idnb.full, (idnb_c, idnb_c[:, :]))
    k.dma("sync", qsc[:, 0:1], (qscol_in, qscol_in[:, :]))
    k.dma("sync", qsc[:, 1:2], (kscol_in, kscol_in[:, :]))
    k.dma("sync", ones2.full, (ones2c_c, ones2c_c[:, :]))
    k.dma("sync", ones2r.full, (ones2r_c, ones2r_c[:, :]))
    load_w_fm(k, wqkv_in, D, 1536, wstage, wqkvb)
    load_w_fm(k, wo_in, 512, D, wstage, wob)

    nseg_in = ceil_div(T_in, 512)
    for sgi in range(nseg_in):
        t0 = sgi * 512
        tw = min(512, T_in - t0)
        if with_ln:
            mean, rstd = emit_ln_stream(k, ctx, x_in, t0, tw, rows, xstage, sq_f)
            emit_bcast_rows(k, ctx, mean, tw, aB)
            emit_bcast_rows(k, ctx, rstd, tw, bB)
        for ci in range(NCHUNK):
            st = xstage[ci % 2]
            k.dma("sync", st[:, 0:tw],
                  (x_in, x_in[ci * 128:(ci + 1) * 128, t0:t0 + tw]),
                  **({"allow_slow_non_contiguous": True} if tw == 1 else {}))
            hst = sq_f[ci % 2]
            if with_ln:
                k.op("vector", "tensor_tensor", [hst[:, 0:tw]],
                     [st[:, 0:tw], aB[:, 0:tw]],
                     hst[:, 0:tw], st[:, 0:tw], aB[:, 0:tw], op=ALU.subtract)
                k.op("vector", "tensor_tensor", [hst[:, 0:tw]],
                     [hst[:, 0:tw], bB[:, 0:tw]],
                     hst[:, 0:tw], hst[:, 0:tw], bB[:, 0:tw], op=ALU.mult)
            else:
                k.op("vector", "tensor_copy", [hst[:, 0:tw]], [st[:, 0:tw]],
                     hst[:, 0:tw], st[:, 0:tw])
            k.dma("gpsimd", (zd, zd[ci * 128:(ci + 1) * 128, t0:t0 + tw]),
                  hst[:, 0:tw], **({"allow_slow_non_contiguous": True}
                                   if tw == 1 else {}))

    def z_shift_stage(ci, t0, tw, slot):
        st = xstage[slot % 2]
        src0 = t0 - 1 if ci >= 4 else t0
        k.dma("sync", st[:, 0:tw], (zd, zd[ci * 128:(ci + 1) * 128,
                                           src0:src0 + tw]))
        hst = sq_f[slot % 2]
        k.op("vector", "tensor_copy", [hst[:, 0:tw]], [st[:, 0:tw]],
             hst[:, 0:tw], st[:, 0:tw])
        return hst[:, 0:tw]

    zstb = [k.sb("zstb0", 128, 512, BF16), k.sb("zstb1", 128, 512, BF16)]

    def norm_rope(seg_w, raw, dst, dst_c0, qk, tab_c, tab_s):
        sqv = sq_f[0][:, 0:seg_w]
        k.op("scalar", "activation", [sqv], [raw], sqv, raw, AF.Square)
        sqb = krs[:, 0:seg_w] if raw is not krs[:, 0:seg_w] else wsc[:, 0:seg_w]
        k.op("vector", "tensor_copy", [wsc[:, 0:seg_w]], [sqv],
             wsc[:, 0:seg_w], sqv)
        pS = ctx.psum("pC")[0:2, 0:seg_w]
        k.op("tensor", "matmul", [pS], [ones2[:, 0:2], wsc[:, 0:seg_w]],
             pS, ones2[:, 0:2], wsc[:, 0:seg_w], start=True, stop=True)
        sr = ssr[0:2, 0:seg_w]
        k.op("vector", "tensor_scalar", [sr], [pS], sr, pS, 1e-18,
             scalar2=None, op0=ALU.max)
        k.op("scalar", "activation", [sr], [sr], sr, sr, AF.Ln)
        k.op("scalar", "activation", [sr], [sr], sr, sr, AF.Exp,
             scale=np.float32(0.5))
        k.op("vector", "tensor_scalar", [sr], [sr], sr, sr, 1e-12,
             scalar2=None, op0=ALU.max)
        k.op("vector", "reciprocal", [sr], [sr], sr, sr)
        pB = ctx.psum("pD")[0:128, 0:seg_w]
        k.op("tensor", "matmul", [pB], [ones2r[0:2, 0:128], sr],
             pB, ones2r[0:2, 0:128], sr, start=True, stop=True)
        k.op("vector", "scalar_tensor_tensor", [kbn[:, 0:seg_w]],
             [raw, pB, qsc[:, qk:qk + 1]],
             kbn[:, 0:seg_w], raw, qsc[:, qk:qk + 1], pB,
             op0=ALU.mult, op1=ALU.mult)
        k.op("vector", "tensor_tensor", [wsc[:, 0:seg_w]], [kbn[:, 0:seg_w], tab_s],
             wsc[:, 0:seg_w], kbn[:, 0:seg_w], tab_s, op=ALU.mult)
        for g in range(2):
            b = g * 64
            k.op("vector", "tensor_scalar", [wsh[b:b + 32, 0:seg_w]],
                 [wsc[:, 0:seg_w]],
                 wsh[b:b + 32, 0:seg_w], wsc[b + 32:b + 64, 0:seg_w], -1.0,
                 scalar2=None, op0=ALU.mult)
            k.op("vector", "tensor_copy", [wsh[b + 32:b + 64, 0:seg_w]],
                 [wsc[:, 0:seg_w]],
                 wsh[b + 32:b + 64, 0:seg_w], wsc[b:b + 32, 0:seg_w])
        k.op("vector", "tensor_tensor", [krs[:, 0:seg_w]],
             [kbn[:, 0:seg_w], tab_c],
             krs[:, 0:seg_w], kbn[:, 0:seg_w], tab_c, op=ALU.mult)
        k.op("vector", "tensor_tensor", [dst], [krs[:, 0:seg_w], wsh[:, 0:seg_w]],
             dst, krs[:, 0:seg_w], wsh[:, 0:seg_w], op=ALU.add)

    NKSEG = TK // 512
    for p in range(4):
        for sgi in range(NKSEG):
            t0 = OFF + sgi * 512
            pvK = ctx.psum("pA")[0:128, 0:512]
            in_q = t0 >= QS
            if in_q:
                pvQ = ctx.psum("pE")[0:128, 0:512]
            vbanks = ("pB", "pC", "pD", "pF")
            for ci in range(NCHUNK):
                hz = z_shift_stage(ci, t0, 512, ci)
                zb_ = zstb[ci % 2]
                k.op("vector", "tensor_copy", [zb_[:, 0:512]], [hz],
                     zb_[:, 0:512], hz)
                k.op("tensor", "matmul", [pvK],
                     [wqkvb.chunk(ci, 512 + p * 128, 512 + (p + 1) * 128),
                      zb_[:, 0:512]],
                     pvK, wqkvb.chunk(ci, 512 + p * 128, 512 + (p + 1) * 128),
                     zb_[:, 0:512], start=(ci == 0), stop=(ci == NCHUNK - 1))
                for tl in range(4):
                    pvV = ctx.psum(vbanks[tl])[0:128, 0:128]
                    k.op("tensor", "matmul", [pvV],
                         [zb_[:, tl * 128:(tl + 1) * 128],
                          wqkvb.chunk(ci, 1024 + p * 128, 1024 + (p + 1) * 128)],
                         pvV, zb_[:, tl * 128:(tl + 1) * 128],
                         wqkvb.chunk(ci, 1024 + p * 128, 1024 + (p + 1) * 128),
                         start=(ci == 0), stop=(ci == NCHUNK - 1))
                if in_q:
                    k.op("tensor", "matmul", [pvQ],
                         [wqkvb.chunk(ci, p * 128, (p + 1) * 128),
                          zb_[:, 0:512]],
                         pvQ, wqkvb.chunk(ci, p * 128, (p + 1) * 128),
                         zb_[:, 0:512], start=(ci == 0), stop=(ci == NCHUNK - 1))
            for tl in range(4):
                pvV = ctx.psum(vbanks[tl])
                vti = sgi * 4 + tl
                for hs in range(2):
                    h = p * 2 + hs
                    vb = vti * 520 + h * 65
                    k.op("vector", "tensor_copy",
                         [vall[0:128, vb:vb + 64]], [pvV[0:128, 0:128]],
                         vall[0:128, vb:vb + 64],
                         pvV[0:128, hs * 64:hs * 64 + 64])
            k.op("vector", "tensor_copy", [krs[:, 0:512]], [pvK],
                 krs[:, 0:512], pvK)
            k.dma("scalar", tabs[:, 0:512], (tab_in["ck"],
                  tab_in["ck"][:, t0:t0 + 512]))
            k.dma("scalar", tabs[:, 512:1024], (tab_in["sksw"],
                  tab_in["sksw"][:, t0:t0 + 512]))
            norm_rope(512, krs[:, 0:512], k2res[:, t0:t0 + 512], 0, 1,
                      tabs[:, 0:512], tabs[:, 512:1024])
            if in_q:
                k.op("vector", "tensor_copy", [krs[:, 0:512]], [pvQ],
                     krs[:, 0:512], pvQ)
                k.dma("scalar", tabs[:, 0:512], (tab_in["cq"],
                      tab_in["cq"][:, t0:t0 + 512]))
                k.dma("scalar", tabs[:, 512:1024], (tab_in["sqsw"],
                      tab_in["sqsw"][:, t0:t0 + 512]))
                norm_rope(512, krs[:, 0:512], q2res[:, t0 - QS:t0 - QS + 512],
                          0, 0, tabs[:, 0:512], tabs[:, 512:1024])
        for hs in range(2):
            h = p * 2 + hs
            for vti in range(TK // 128):
                vb = vti * 520 + h * 65
                k.op("gpsimd", "memset", [vall[0:128, vb + 64:vb + 65]], [],
                     vall[0:128, vb + 64:vb + 65], 1.0)

        for hs in range(2):
            h = p * 2 + hs
            k.op("vector", "tensor_copy", [kh[0:64, OFF:OFF + TK]],
                 [k2res[:, OFF:OFF + TK]],
                 kh[0:64, OFF:OFF + TK], k2res[hs * 64:hs * 64 + 64, OFF:OFF + TK])
            k.dma("scalar", kh[64:65, :], (onesb_c, onesb_c[0:1, 0:T_in]))
            k.op("vector", "tensor_copy", [qh[0:64, :]], [q2res.full],
                 qh[0:64, :], q2res[hs * 64:hs * 64 + 64, 0:TQ])
            k.op("gpsimd", "memset", [qh[64:65, :]], [], qh[64:65, :], 0.0)
            for qt in range(TQ // 128):
                w = qt // 4 + 1
                kstart = OFF + (w - 1) * 512
                kend = OFF + w * 512 + (qt % 4 + 1) * 128
                first = True
                si = 0
                for kt0 in range(kstart, kend, 512):
                    kw = min(512, kend - kt0)
                    ps1 = ctx.psum(("pA", "pB")[si % 2])
                    pv1 = ps1[0:128, 0:kw]
                    k.op("tensor", "matmul", [pv1],
                         [qh[0:64, qt * 128:(qt + 1) * 128],
                          kh[0:64, kt0:kt0 + kw]],
                         pv1, qh[0:64, qt * 128:(qt + 1) * 128],
                         kh[0:64, kt0:kt0 + kw], start=True, stop=True)
                    k.op("vector", "tensor_scalar",
                         [scr[si % 2][:, 0:kw], macc[:, qt:qt + 1]],
                         [pv1] + ([] if first else [macc[:, qt:qt + 1]]),
                         scr[si % 2][:, 0:kw], pv1, 1.0,
                         scalar2=(-1e30 if first else macc[:, qt:qt + 1]),
                         op0=ALU.mult, op1=ALU.max,
                         accum_out=macc[:, qt:qt + 1])
                    first = False
                    si += 1
            k.op("vector", "tensor_copy", [mbf.full], [macc.full],
                 mbf.full, macc.full)
            for qt in range(TQ // 128):
                pt = pT[qt % 2]
                k.op("tensor", "transpose", [pt[0:1, 0:128]],
                     [mbf[:, qt:qt + 1], idnb.full],
                     pt[0:1, 0:128], mbf[:, qt:qt + 1], idnb.full)
                k.op("vector", "tensor_scalar",
                     [qh[64:65, qt * 128:(qt + 1) * 128]], [pt[0:1, 0:128]],
                     qh[64:65, qt * 128:(qt + 1) * 128], pt[0:1, 0:128], -1.0,
                     scalar2=None, op0=ALU.mult)
            for w in range(1, NQW + 1):
                q0 = (w - 1) * 512
                stages = []
                for kti in range(8):
                    kt = OFF + (w - 1) * 512 + kti * 128
                    ps2 = ctx.psum(("pC", "pD")[kti % 2])
                    pv2 = ps2[0:128, 0:512]
                    k.op("tensor", "matmul", [pv2],
                         [kh[0:65, kt:kt + 128], qh[0:65, q0:q0 + 512]],
                         pv2, kh[0:65, kt:kt + 128], qh[0:65, q0:q0 + 512],
                         start=True, stop=True)
                    stages.append((kti, kt, pv2))
                    if len(stages) == 2:
                        _emit_exp_pv_win(k, stages.pop(0), h, w, FMW, att,
                                         cmk, pmk, pmk4, vall, pE, OFF)
                _emit_exp_pv_win(k, stages.pop(0), h, w, FMW, att, cmk, pmk,
                                 pmk4, vall, pE, OFF)
                k.op("vector", "reciprocal", [recd[0:1, 0:512]], [pE.full],
                     recd[0:1, 0:512], pE[64:65, 0:512])
                k.op("tensor", "matmul", [pF.full],
                     [ctx.ones_row_f[0:1, 0:64], recd[0:1, 0:512]],
                     pF[0:64, 0:512], ctx.ones_row_f[0:1, 0:64],
                     recd[0:1, 0:512], start=True, stop=True)
                k.op("vector", "tensor_copy", [recB.full], [pF.full],
                     recB[0:64, 0:512], pF[0:64, 0:512])
                k.op("vector", "tensor_tensor", [cseg.full],
                     [pE.full, recB.full],
                     cseg[0:64, 0:512], pE[0:64, 0:512], recB[0:64, 0:512],
                     op=ALU.mult)
                k.op("gpsimd", "tensor_copy",
                     [oall.chunk(h // 2, q0, q0 + 512, p0=hs * 64,
                                 p1=hs * 64 + 64)], [cseg.full],
                     oall.chunk(h // 2, q0, q0 + 512, p0=hs * 64,
                                p1=hs * 64 + 64), cseg[0:64, 0:512])

    ii = 0
    for co in range(NCHUNK):
        for t0 in range(0, TQ, 512):
            ps = ctx.psum(("pA", "pB")[ii % 2])
            ii += 1
            pv = ps[0:128, 0:512]
            for ci in range(4):
                k.op("tensor", "matmul", [pv],
                     [wob.chunk(ci, co * 128, (co + 1) * 128),
                      oall.chunk(ci, t0, t0 + 512)],
                     pv, wob.chunk(ci, co * 128, (co + 1) * 128),
                     oall.chunk(ci, t0, t0 + 512),
                     start=(ci == 0), stop=(ci == 3))
            st = xstage[ii % 2]
            k.dma("scalar", st[:, 0:512],
                  (zd, zd[co * 128:(co + 1) * 128, QS + t0:QS + t0 + 512]))
            y = sq_f[ii % 2]
            k.op("vector", "tensor_tensor", [y[:, 0:512]], [pv, st[:, 0:512]],
                 y[:, 0:512], pv, st[:, 0:512], op=ALU.add)
            k.dma("gpsimd", (x1_out, x1_out[co * 128:(co + 1) * 128,
                                            t0:t0 + 512]), y[:, 0:512])
    k.seal()
    return k


def _emit_exp_pv_win(k, stage, h, w, FMW, att, cmk, pmk, pmk4, vall, pE, OFF):
    kti, kt, pv2 = stage
    a = att[kti % len(att)]
    k.op("scalar", "activation", [a.full], [pv2], a[0:128, 0:512], pv2, AF.Exp)
    if kti < 4:
        mk = pmk4 if w == FMW else pmk
        p = kti
    else:
        mk = cmk
        p = kti - 4
    k.op("vector", "tensor_tensor", [a.full],
         [a.full, mk[:, p * 512:(p + 1) * 512]],
         a[0:128, 0:512], a[0:128, 0:512], mk[:, p * 512:(p + 1) * 512],
         op=ALU.mult)
    vti = (kt - OFF) // 128
    vb = vti * 520 + h * 65
    k.op("tensor", "matmul", [pE.full],
         [vall[0:128, vb:vb + 65], a.full],
         pE[0:65, 0:512], vall[0:128, vb:vb + 65], a[0:128, 0:512],
         start=(kti == 0), stop=(kti == 7))


def build_lwinB_prog(TF, with_resid, coef=0.5, T_OWN=2048, name="lwinB"):
    HID2, HID = 5460, 2730
    NCO = ceil_div(HID, 128)
    k = Prog(name)
    ctx = Ctx(k)
    T_OUT = T_OWN if with_resid else TF
    OUT0 = TF - T_OWN if with_resid else 0
    x1_in = k.inp("x1", (D, TF + 1))
    lg_in = k.inp("lg", (D, 1))
    lb_in = k.inp("lb", (D, 1))
    w1_in = k.inp("w1", (D, HID2))
    b1_in = k.inp("b1", (HID2, 1))
    w2_in = k.inp("w2", (HID, D))
    b2_in = k.inp("b2", (D, 1))
    if with_resid:
        xmac_in = k.inp("xmac", (D, T_OWN))
    y_out = k.out("y", (D, T_OUT))

    rows = k.sb("rows", 1, 3 * 512, F32)
    xstage = [k.sb("xst0", 128, 512, F32), k.sb("xst1", 128, 512, F32)]
    sq_f = [k.sb("sqf0", 128, 512, F32), k.sb("sqf1", 128, 512, F32)]
    aB = k.sb("aB", 128, 512, F32)
    bB = k.sb("bB", 128, 512, F32)
    wstage = [k.sb("wsg0", 128, 1024, F32), k.sb("wsg1", 128, 1024, F32)]
    w1b = FM(k, "w1b", D, HID2, BF16)
    w2b = FM(k, "w2b", HID, D, BF16)
    hsb = FM(k, "hsb", D, 512, BF16)
    m_all = k.sb("m_all", 128, NCO * 512, BF16)
    gel = k.sb("gel", 128, 512, BF16)
    lgc = k.sb("lgc", 128, NCHUNK, F32)
    lbc = k.sb("lbc", 128, NCHUNK, F32)
    b1c = k.sb("b1c", 128, 2 * NCO, F32)
    b2c = k.sb("b2c", 128, NCHUNK, F32)
    yst = [k.sb("yst0", 128, 512, F32), k.sb("yst1", 128, 512, F32)]

    for ci in range(NCHUNK):
        k.dma("sync", lgc[:, ci:ci + 1], (lg_in, lg_in[ci * 128:(ci + 1) * 128, :]))
        k.dma("sync", lbc[:, ci:ci + 1], (lb_in, lb_in[ci * 128:(ci + 1) * 128, :]))
        k.dma("sync", b2c[:, ci:ci + 1], (b2_in, b2_in[ci * 128:(ci + 1) * 128, :]))
    for co in range(NCO):
        cow = min(128, HID - co * 128)
        k.dma("sync", b1c[0:cow, 2 * co:2 * co + 1],
              (b1_in, b1_in[co * 128:co * 128 + cow, :]))
        k.dma("sync", b1c[0:cow, 2 * co + 1:2 * co + 2],
              (b1_in, b1_in[HID + co * 128:HID + co * 128 + cow, :]))
    for ci in range(NCHUNK):
        for piece in range(6):
            c0 = piece * 1024
            cw = min(1024, HID2 - c0)
            st = wstage[piece % 2]
            k.dma("sync", st[:, 0:cw],
                  (w1_in, w1_in[ci * 128:(ci + 1) * 128, c0:c0 + cw]))
            k.op("vector", "tensor_copy", [w1b.chunk(ci, c0, c0 + cw)],
                 [st[:, 0:cw]], w1b.chunk(ci, c0, c0 + cw), st[:, 0:cw])
    for co in range(NCO):
        cow = min(128, HID - co * 128)
        st = wstage[co % 2]
        k.dma("sync", st[0:cow, 0:1024],
              (w2_in, w2_in[co * 128:co * 128 + cow, :]))
        k.op("vector", "tensor_copy", [w2b.chunk(co, p1=cow)],
             [st[0:cow, 0:1024]], w2b.chunk(co, p1=cow), st[0:cow, 0:1024])

    coff = lambda ci: 0 if ci >= 4 else 1
    for t0 in range(0, TF, 512):
        tw = min(512, TF - t0)
        mean, rstd = emit_ln_stream(k, ctx, x1_in, t0, tw, rows, xstage, sq_f,
                                    chunk_off=coff)
        emit_bcast_rows(k, ctx, mean, tw, aB)
        emit_bcast_rows(k, ctx, rstd, tw, bB)
        for ci in range(NCHUNK):
            st = xstage[ci % 2]
            k.dma("sync", st[:, 0:tw],
                  (x1_in, x1_in[ci * 128:(ci + 1) * 128,
                                t0 + coff(ci):t0 + coff(ci) + tw]))
            hst = sq_f[ci % 2]
            k.op("vector", "tensor_tensor", [hst[:, 0:tw]],
                 [st[:, 0:tw], aB[:, 0:tw]],
                 hst[:, 0:tw], st[:, 0:tw], aB[:, 0:tw], op=ALU.subtract)
            k.op("vector", "tensor_tensor", [hst[:, 0:tw]],
                 [hst[:, 0:tw], bB[:, 0:tw]],
                 hst[:, 0:tw], hst[:, 0:tw], bB[:, 0:tw], op=ALU.mult)
            k.op("gpsimd", "tensor_scalar", [hsb.chunk(ci, 0, tw)],
                 [hst[:, 0:tw], lgc[:, ci:ci + 1], lbc[:, ci:ci + 1]],
                 hsb.chunk(ci, 0, tw), hst[:, 0:tw], lgc[:, ci:ci + 1],
                 scalar2=lbc[:, ci:ci + 1], op0=ALU.mult, op1=ALU.add)
        for co in range(NCO):
            cow = min(128, HID - co * 128)
            pA_ = ctx.psum("pA")[0:cow, 0:tw]
            pB_ = ctx.psum("pB")[0:cow, 0:tw]
            for ci in range(NCHUNK):
                k.op("tensor", "matmul", [pA_],
                     [w1b.chunk(ci, co * 128, co * 128 + cow),
                      hsb.chunk(ci, 0, tw)],
                     pA_, w1b.chunk(ci, co * 128, co * 128 + cow),
                     hsb.chunk(ci, 0, tw),
                     start=(ci == 0), stop=(ci == NCHUNK - 1))
            for ci in range(NCHUNK):
                k.op("tensor", "matmul", [pB_],
                     [w1b.chunk(ci, HID + co * 128, HID + co * 128 + cow),
                      hsb.chunk(ci, 0, tw)],
                     pB_, w1b.chunk(ci, HID + co * 128, HID + co * 128 + cow),
                     hsb.chunk(ci, 0, tw),
                     start=(ci == 0), stop=(ci == NCHUNK - 1))
            k.op("scalar", "activation", [gel[0:cow, 0:tw]],
                 [pB_, b1c[0:cow, 2 * co + 1:2 * co + 2]],
                 gel[0:cow, 0:tw], pB_, AF.Gelu,
                 bias=b1c[0:cow, 2 * co + 1:2 * co + 2])
            mv = m_all[0:cow, co * 512:co * 512 + tw]
            k.op("vector", "scalar_tensor_tensor", [mv],
                 [pA_, gel[0:cow, 0:tw], b1c[0:cow, 2 * co:2 * co + 1]],
                 mv, pA_, b1c[0:cow, 2 * co:2 * co + 1], gel[0:cow, 0:tw],
                 op0=ALU.add, op1=ALU.mult)
        for outco in range(NCHUNK):
            ps = ctx.psum(("pC", "pD")[outco % 2])
            pv = ps[0:128, 0:tw]
            for co in range(NCO):
                cow = min(128, HID - co * 128)
                k.op("tensor", "matmul", [pv],
                     [w2b.chunk(co, outco * 128, (outco + 1) * 128, p1=cow),
                      m_all[0:cow, co * 512:co * 512 + tw]],
                     pv, w2b.chunk(co, outco * 128, (outco + 1) * 128, p1=cow),
                     m_all[0:cow, co * 512:co * 512 + tw],
                     start=(co == 0), stop=(co == NCO - 1))
            st = xstage[outco % 2]
            k.dma("scalar", st[:, 0:tw],
                  (x1_in, x1_in[outco * 128:(outco + 1) * 128,
                                t0 + 1:t0 + 1 + tw]))
            y = yst[outco % 2]
            k.op("vector", "scalar_tensor_tensor", [y[:, 0:tw]],
                 [pv, st[:, 0:tw], b2c[:, outco:outco + 1]],
                 y[:, 0:tw], pv, b2c[:, outco:outco + 1], st[:, 0:tw],
                 op0=ALU.add, op1=ALU.add)
            if with_resid:
                o0 = t0 - OUT0
                lo = max(0, -o0)
                if lo >= tw:
                    continue
                stm = sq_f[outco % 2]
                nrw = ({"allow_slow_non_contiguous": True}
                       if tw - lo == 1 else {})
                k.dma("sync", stm[:, lo:tw],
                      (xmac_in, xmac_in[outco * 128:(outco + 1) * 128,
                                        o0 + lo:o0 + tw]), **nrw)
                k.op("vector", "scalar_tensor_tensor", [y[:, lo:tw]],
                     [y[:, lo:tw], stm[:, lo:tw]],
                     y[:, lo:tw], y[:, lo:tw], float(coef), stm[:, lo:tw],
                     op0=ALU.mult, op1=ALU.add)
                k.dma("gpsimd", (y_out, y_out[outco * 128:(outco + 1) * 128,
                                              o0 + lo:o0 + tw]), y[:, lo:tw],
                      **nrw)
            else:
                k.dma("gpsimd", (y_out, y_out[outco * 128:(outco + 1) * 128,
                                              t0:t0 + tw]), y[:, 0:tw])
    k.seal()
    return k


_PROG_CACHE = {}


def get_prog(name, builder, *args, **kwargs):
    if name not in _PROG_CACHE:
        _PROG_CACHE[name] = builder(*args, name=name, **kwargs)
    return _PROG_CACHE[name]


def lwin_tables(T_in, gstart):
    t = gstart + np.arange(T_in, dtype=np.float64)
    half_idx = np.arange(0, 64, 2, dtype=np.float64)
    inv_freq = 1.0 / 10000.0 ** (half_idx / 64.0)
    freqs = np.concatenate([inv_freq, inv_freq])[:, None] * t[None, :]
    base = (half_idx + 0.4 * 64) / (1.4 * 64)
    base = np.concatenate([base, base])[:, None]
    power = (t - 4096 // 2) / float(512 // 2)
    scale = base ** power[None, :]
    cos, sin = np.cos(freqs), np.sin(freqs)
    cq = np.tile(cos * scale * 8.0, (2, 1))
    sq = np.tile(sin * scale * 8.0, (2, 1))
    ck = np.tile(cos / scale, (2, 1))
    sk = np.tile(sin / scale, (2, 1))
    return tuple(bf16_np(v) for v in (cq, sq, ck, sk))


def _pmask_np():
    pm = np.zeros((128, 4 * 512))
    for p in range(4):
        qi = np.arange(512)[None, :]
        kj = np.arange(128)[:, None]
        pm[:, p * 512:(p + 1) * 512] = (qi <= kj + p * 128)
    return pm


def run_lwin_call(xT, wl, coef=0.5, tag="l0f"):
    B_ = len(xT)
    D_ = 1024
    kf1 = get_prog("lwinA_f1", build_lwinA_prog, 4097, 7, 1, True, 4)
    kf1b = get_prog("lwinB_f1", build_lwinB_prog, 3583, False)
    kf2 = get_prog("lwinA_f2", build_lwinA_prog, 3074, 5, 2, False, 2)
    kf2b = get_prog("lwinB_f2", build_lwinB_prog, 2559, True, coef=coef)
    pm = _pmask_np()

    def core_inputs(kind, l, c, xsrc, xmac=None):
        b, sH = c // 2, c % 2
        start = sH * 2048
        if kind == "f1A":
            T_in, g0 = 4097, start - 2049
        else:
            T_in, g0 = 3074, start - 1026
        lo, hi = g0, g0 + T_in
        buf = np.zeros((D_, T_in), np.float32)
        lo_c = max(lo, 0)
        buf[:, lo_c - lo:T_in - (hi - min(hi, 4096))] = xsrc[b][:, lo_c:min(hi, 4096)]
        cq, sq, ck, sk = lwin_tables(T_in, g0)
        d = {
            "x": buf,
            "wqkv": wl["wqkv"][l],
            "qscol": np.ascontiguousarray(
                np.tile(wl["qs"][l], 2).reshape(128, 1).astype(np.float32)),
            "kscol": np.ascontiguousarray(
                np.tile(wl["ks"][l], 2).reshape(128, 1).astype(np.float32)),
            "cq": cq, "sqsw": sq, "ck": ck, "sksw": sk,
            "pmask4": bf16_np(np.zeros((128, 4 * 512)) if sH == 0 else pm),
            "wo": wl["wo"][l],
        }
        return d

    def binputs(l, c, x1, xmac=None):
        d = {
            "x1": x1,
            "lg": wl["lg"][l].reshape(D_, 1), "lb": wl["lb"][l].reshape(D_, 1),
            "w1": wl["w1"][l], "b1": wl["b1"][l].reshape(-1, 1),
            "w2": wl["w2"][l], "b2": wl["b2"][l].reshape(D_, 1),
        }
        if xmac is not None:
            d["xmac"] = xmac
        return d

    r1 = kf1.run([core_inputs("f1A", 0, c, xT) for c in range(2 * B_)])
    r1b = kf1b.run([binputs(0, c, r1[c]["x1"]) for c in range(2 * B_)])
    y1 = [np.zeros((D_, 4096), np.float32) for _ in range(B_)]
    for c in range(2 * B_):
        b, sH = c // 2, c % 2
        g0 = sH * 2048 - 1535
        v = r1b[c]["y"]
        lo = max(g0, 0)
        if sH == 0:
            y1[b][:, lo:g0 + 3583] = v[:, lo - g0:]
        else:
            y1[b][:, 2048:4096] = v[:, 3583 - 2048:]
    def f2x(c):
        b, sH = c // 2, c % 2
        g0 = sH * 2048 - 1026
        buf = np.zeros((D_, 3074), np.float32)
        lo = max(g0, 0)
        buf[:, lo - g0:] = y1[b][:, lo:g0 + 3074]
        return buf
    in2 = []
    for c in range(2 * B_):
        d = core_inputs("f2A", 1, c, [np.zeros((D_, 4096), np.float32)] * B_)
        d["x"] = f2x(c)
        in2.append(d)
    r2 = kf2.run(in2)
    xm = [np.ascontiguousarray(xT[c // 2][:, (c % 2) * 2048:(c % 2) * 2048 + 2048])
          for c in range(2 * B_)]
    r2b = kf2b.run([binputs(1, c, r2[c]["x1"], xm[c]) for c in range(2 * B_)])
    out = [np.zeros((D_, 4096), np.float32) for _ in range(B_)]
    for c in range(2 * B_):
        b, sH = c // 2, c % 2
        out[b][:, sH * 2048:(sH + 1) * 2048] = r2b[c]["y"]
    return out


def build_l3attn_prog(T_OWN=2048, name="l3attn"):
    H, DK = 16, 64
    k = Prog(name)
    ctx = Ctx(k)
    x_in = k.inp("x", (D, T_OWN))
    w_in = {n: k.inp(n, (D, D)) for n in ("wq", "wk", "wv", "wout")}
    b_in = {n: k.inp(n, (1, D), F32) for n in ("bq", "bk", "bv")}
    bout_in = k.inp("bout", (D, 1))
    idnb_c = k.const("idnb3", bf16_np(np.eye(128)))
    y_out = k.out("y", (D, T_OWN))

    hbf = FM(k, "hbf", D, T_OWN, BF16)
    rows = k.sb("rows", 1, 3 * 512, F32)
    xstage = [k.sb("xst0", 128, 512, F32), k.sb("xst1", 128, 512, F32)]
    sq_f = [k.sb("sqf0", 128, 512, F32), k.sb("sqf1", 128, 512, F32)]
    aB = k.sb("aB", 128, 512, F32)
    bB = k.sb("bB", 128, 512, F32)
    wstage = [k.sb("wsg0", 128, 1024, F32), k.sb("wsg1", 128, 1024, F32)]
    wb = {n: FM(k, n + "b", D, D, BF16) for n in ("wq", "wk", "wv", "wout")}
    brow = k.sb("brow", 1, 3 * D, BF16)
    qt_ = k.sb("qt", 128, D, BF16)
    kt_ = k.sb("kt", 128, D, BF16)
    vt_ = k.sb("vt", 128, D, BF16)
    tmp = k.sb("tmp", 128, 1024, F32)
    sc = k.sb("sc", 128, 256, F32)
    rmx = k.sb("rmx", 128, 16, F32)
    den = k.sb("den", 128, 16, F32)
    ct = k.sb("ct", 128, D, BF16)
    ctt = FM(k, "ctt", D, T_OWN, BF16)
    idnb = k.sb("idnb3s", 128, 128, BF16)
    bcol = k.sb("bcol", 128, NCHUNK, F32)
    yst = [k.sb("yst0", 128, 512, F32), k.sb("yst1", 128, 512, F32)]
    pT = [k.ps("pT0", 128, 1024, BF16), k.ps("pT1", 128, 1024, BF16)]

    k.dma("sync", idnb.full, (idnb_c, idnb_c[:, :]))
    for n in ("wq", "wk", "wv", "wout"):
        load_w_fm(k, w_in[n], D, D, wstage, wb[n])
    for i, n in enumerate(("bq", "bk", "bv")):
        k.dma("sync", wstage[0][0:1, 0:D], (b_in[n], b_in[n][0:1, :]))
        k.op("vector", "tensor_copy", [brow[0:1, i * D:(i + 1) * D]],
             [wstage[0][0:1, 0:D]],
             brow[0:1, i * D:(i + 1) * D], wstage[0][0:1, 0:D])
    for ci in range(NCHUNK):
        k.dma("sync", bcol[:, ci:ci + 1], (bout_in, bout_in[ci * 128:(ci + 1) * 128, :]))
    onesb = k.sb("onesb", 1, 128, BF16)
    k.op("gpsimd", "memset", [onesb.full], [], onesb.full, 1.0)

    emit_ln_full(k, ctx, x_in, T_OWN, hbf, rows, xstage, sq_f, aB, bB, TM=512)

    for tt in range(T_OWN // 128):
        t0 = tt * 128
        for dst, wn, bi, scl in ((qt_, "wq", 0, 0.125), (kt_, "wk", 1, None),
                                 (vt_, "wv", 2, None)):
            for half in range(2):
                ps = ctx.psum(("pA", "pB")[half])
                pv = ps[0:128, 0:512]
                for ci in range(NCHUNK):
                    k.op("tensor", "matmul", [pv],
                         [hbf.chunk(ci, t0, t0 + 128),
                          wb[wn].chunk(ci, half * 512, (half + 1) * 512)],
                         pv, hbf.chunk(ci, t0, t0 + 128),
                         wb[wn].chunk(ci, half * 512, (half + 1) * 512),
                         start=(ci == 0), stop=False)
                k.op("tensor", "matmul", [pv],
                     [onesb[0:1, 0:128],
                      brow[0:1, bi * D + half * 512:bi * D + (half + 1) * 512]],
                     pv, onesb[0:1, 0:128],
                     brow[0:1, bi * D + half * 512:bi * D + (half + 1) * 512],
                     start=False, stop=True)
                dv = dst[0:128, half * 512:(half + 1) * 512]
                if scl is not None:
                    k.op("vector", "tensor_scalar", [dv], [pv], dv, pv,
                         scl, scalar2=None, op0=ALU.mult)
                else:
                    k.op("vector", "tensor_copy", [dv], [pv], dv, pv)
        for h in range(16):
            qb = qt_.rawv(bass.AP(qt_.h, h * DK, [[qt_.free, 128], [0, 16],
                                                  [1, DK]]))
            k.op("vector", "tensor_tensor", [tmp.full], [qt_.full, kt_.full],
                 tmp.full, qb.ap, kt_.h[0:128, 0:D], op=ALU.mult)
            k.op("vector", "tensor_reduce",
                 [sc[0:128, h * 16:(h + 1) * 16]], [tmp.full],
                 sc[0:128, h * 16:(h + 1) * 16],
                 bass.AP(tmp.h, 0, [[tmp.free, 128], [DK, 16], [1, DK]]),
                 op=ALU.add, axis=AX.X)
        k.op("vector", "tensor_reduce", [rmx.full], [sc.full],
             rmx.full, bass.AP(sc.h, 0, [[sc.free, 128], [16, 16], [1, 16]]),
             op=ALU.max, axis=AX.X)
        k.op("vector", "tensor_tensor", [sc.full], [sc.full, rmx.full],
             sc.h[0:128, 0:256], bass.AP(sc.h, 0, [[sc.free, 128], [16, 16], [1, 16]]),
             bass.AP(rmx.h, 0, [[rmx.free, 128], [1, 16], [0, 16]]),
             op=ALU.subtract)
        k.op("scalar", "activation", [sc.full], [sc.full],
             sc.full, sc.full, AF.Exp)
        k.op("vector", "tensor_reduce", [den.full], [sc.full],
             den.full, bass.AP(sc.h, 0, [[sc.free, 128], [16, 16], [1, 16]]),
             op=ALU.add, axis=AX.X)
        k.op("vector", "reciprocal", [den.full], [den.full],
             den.full, den.full)
        k.op("vector", "tensor_tensor", [sc.full], [sc.full, den.full],
             sc.h[0:128, 0:256], bass.AP(sc.h, 0, [[sc.free, 128], [16, 16], [1, 16]]),
             bass.AP(den.h, 0, [[den.free, 128], [1, 16], [0, 16]]),
             op=ALU.mult)
        for h in range(16):
            for g in range(16):
                cv = ct[0:128, h * DK:(h + 1) * DK]
                if g == 0:
                    k.op("vector", "tensor_scalar", [cv],
                         [vt_.full, sc[0:128, h * 16:h * 16 + 1]],
                         cv, vt_[0:128, 0:DK],
                         sc[0:128, h * 16 + g:h * 16 + g + 1],
                         scalar2=None, op0=ALU.mult)
                else:
                    k.op("vector", "scalar_tensor_tensor", [cv],
                         [vt_.full, sc.full, cv],
                         cv, vt_[0:128, g * DK:(g + 1) * DK],
                         sc[0:128, h * 16 + g:h * 16 + g + 1], cv,
                         op0=ALU.mult, op1=ALU.add)
        for fb in range(8):
            pt = pT[fb % 2]
            k.op("tensor", "transpose", [pt[0:128, 0:128]],
                 [ct[0:128, fb * 128:(fb + 1) * 128], idnb.full],
                 pt[0:128, 0:128], ct[0:128, fb * 128:(fb + 1) * 128],
                 idnb.full)
            k.op("vector", "tensor_copy", [ctt.chunk(fb, t0, t0 + 128)],
                 [pt[0:128, 0:128]],
                 ctt.chunk(fb, t0, t0 + 128), pt[0:128, 0:128])
    ii = 0
    for co in range(NCHUNK):
        for t0 in range(0, T_OWN, 512):
            ps = ctx.psum(("pA", "pB")[ii % 2])
            ii += 1
            pv = ps[0:128, 0:512]
            for ci in range(NCHUNK):
                k.op("tensor", "matmul", [pv],
                     [wb["wout"].chunk(ci, co * 128, (co + 1) * 128),
                      ctt.chunk(ci, t0, t0 + 512)],
                     pv, wb["wout"].chunk(ci, co * 128, (co + 1) * 128),
                     ctt.chunk(ci, t0, t0 + 512),
                     start=(ci == 0), stop=(ci == NCHUNK - 1))
            st = xstage[ii % 2]
            k.dma("scalar", st[:, 0:512],
                  (x_in, x_in[co * 128:(co + 1) * 128, t0:t0 + 512]))
            y = yst[ii % 2]
            k.op("vector", "scalar_tensor_tensor", [y.full],
                 [pv, st[:, 0:512], bcol[:, co:co + 1]],
                 y.full, pv, bcol[:, co:co + 1], st[:, 0:512],
                 op0=ALU.add, op1=ALU.add)
            k.dma("gpsimd", (y_out, y_out[co * 128:(co + 1) * 128,
                                          t0:t0 + 512]), y.full)
    k.seal()
    return k


def emit_dwconv(k, dst_fm, src_fm, fcols, fci_base, T_out, stride, src_w,
                eng="vector"):
    for ci in range(NCHUNK):
        first = True
        for kk in range(4):
            off = kk - 1
            t_lo = max(0, ceil_div(-off, stride))
            t_hi = min(T_out, (src_w - 1 - off) // stride + 1)
            if t_lo >= t_hi:
                continue
            dv = dst_fm.chunk(ci, t_lo, t_hi)
            src_ap = bass.AP(src_fm.tile.h,
                             ci * src_fm.ncols + stride * t_lo + off,
                             [[src_fm.tile.free, 128],
                              [stride, t_hi - t_lo]])
            srcv = src_fm.tile.rawv(src_ap)
            fc = fcols[:, fci_base + ci * 4 + kk:fci_base + ci * 4 + kk + 1]
            if first:
                k.op(eng, "memset", [dst_fm.chunk(ci, 0, T_out)], [],
                     dst_fm.chunk(ci, 0, T_out), 0.0)
                k.op(eng, "tensor_scalar", [dv], [srcv, fc],
                     dv, src_ap, fc, scalar2=None, op0=ALU.mult)
                first = False
            else:
                k.op("vector", "scalar_tensor_tensor", [dv], [srcv, fc, dv],
                     dv, src_ap, fc, dv, op0=ALU.mult, op1=ALU.add)


def emit_dwconv_up(k, dst_fm, src_fm, fcols, fci_base, T_out, src_w,
                   first):
    import math as _m
    for ci in range(NCHUNK):
        if first:
            k.op("vector", "memset", [dst_fm.chunk(ci, 0, T_out)], [],
                 dst_fm.chunk(ci, 0, T_out), 0.0)
        for par in range(2):
            for kk in range(4):
                num = par - 1 + kk
                u_off = _m.floor(num / 2.0)
                u_lo = max(0, -u_off)
                u_hi = min(T_out // 2, src_w - u_off)
                if u_lo >= u_hi:
                    continue
                dv_ap = bass.AP(dst_fm.tile.h,
                                ci * dst_fm.ncols + par + 2 * u_lo,
                                [[dst_fm.tile.free, 128], [2, u_hi - u_lo]])
                dvv = dst_fm.tile.rawv(dv_ap)
                src_ap = bass.AP(src_fm.tile.h,
                                 ci * src_fm.ncols + u_off + u_lo,
                                 [[src_fm.tile.free, 128], [1, u_hi - u_lo]])
                srcv = src_fm.tile.rawv(src_ap)
                fc = fcols[:, fci_base + ci * 4 + kk:fci_base + ci * 4 + kk + 1]
                k.op("vector", "scalar_tensor_tensor", [dvv], [srcv, fc, dvv],
                     dv_ap, src_ap, fc, dv_ap, op0=ALU.mult, op1=ALU.add)


def build_l2attn_prog(T=S, name="l2attn"):
    H, DH = 16, 64
    k = Prog(name)
    ctx = Ctx(k)
    x_in = k.inp("x", (D, T))
    filt_in = k.inp("filts", (128, 4 * NCHUNK * 4))
    gcol_in = k.inp("gcol", (D, 1))
    bcol_in = k.inp("bcol", (D, 1))
    idnb_c = k.const("idnb2", bf16_np(np.eye(128)))
    y_out = k.out("y", (D, T))

    hd = k.dram_scratch("hd", (D, T))
    hstage = k.sb("hstage", 128, 2051, F32)
    hfin = k.sb("hfin", 128, 512, F32)
    rows = k.sb("rows", 1, 3 * 512, F32)
    xstage = [k.sb("xst0", 128, 512, F32), k.sb("xst1", 128, 512, F32)]
    sq_f = [k.sb("sqf0", 128, 512, F32), k.sb("sqf1", 128, 512, F32)]
    aB = k.sb("aB", 128, 512, F32)
    bB = k.sb("bB", 128, 512, F32)
    fcols = k.sb("fcols", 128, 4 * NCHUNK * 4, F32)
    low1 = FM(k, "low1", D, T // 2, BF16)
    hi1 = FM(k, "hi1", D, T // 2, BF16)
    low2 = FM(k, "low2", D, T // 4, BF16)
    hi2 = FM(k, "hi2", D, T // 4, BF16)
    qel = FM(k, "qel", D, T // 4, BF16)
    ktok = k.sb("ktok", 128, DH + 1, BF16)
    vtok = k.sb("vtok", 128, DH + 1, BF16)
    kvm = k.sb("kvm", 128, 65, BF16)
    la = FM(k, "la", D, T // 4, BF16)
    recd = k.sb("recd", 1, 512, F32)
    recB = k.sb("recB", 64, 512, F32)
    gcol = k.sb("gcolS", 128, NCHUNK, F32)
    bcol = k.sb("bcolS", 128, NCHUNK, F32)
    idnb = k.sb("idnb2s", 128, 128, BF16)
    yst = [k.sb("yst0", 128, 512, F32), k.sb("yst1", 128, 512, F32)]
    pT = [k.ps("pT0", 128, 1024, BF16), k.ps("pT1", 128, 1024, BF16)]
    pE = ctx.psum("pE")
    pF = ctx.psum("pF")

    k.dma("sync", fcols.full, (filt_in, filt_in[:, :]))
    k.dma("sync", idnb.full, (idnb_c, idnb_c[:, :]))
    for ci in range(NCHUNK):
        k.dma("sync", gcol[:, ci:ci + 1], (gcol_in, gcol_in[ci * 128:(ci + 1) * 128, :]))
        k.dma("sync", bcol[:, ci:ci + 1], (bcol_in, bcol_in[ci * 128:(ci + 1) * 128, :]))

    for t0 in range(0, T, 512):
        mean, rstd = emit_ln_stream(k, ctx, x_in, t0, 512, rows, xstage, sq_f)
        emit_bcast_rows(k, ctx, mean, 512, aB)
        emit_bcast_rows(k, ctx, rstd, 512, bB)
        for ci in range(NCHUNK):
            st = xstage[ci % 2]
            k.dma("sync", st[:, 0:512],
                  (x_in, x_in[ci * 128:(ci + 1) * 128, t0:t0 + 512]))
            hst = sq_f[ci % 2]
            k.op("vector", "tensor_tensor", [hst[:, 0:512]],
                 [st[:, 0:512], aB[:, 0:512]],
                 hst[:, 0:512], st[:, 0:512], aB[:, 0:512], op=ALU.subtract)
            k.op("vector", "tensor_tensor", [hst[:, 0:512]],
                 [hst[:, 0:512], bB[:, 0:512]],
                 hst[:, 0:512], hst[:, 0:512], bB[:, 0:512], op=ALU.mult)
            k.dma("gpsimd", (hd, hd[ci * 128:(ci + 1) * 128, t0:t0 + 512]),
                  hst[:, 0:512])

    for ci in range(NCHUNK):
        for s0 in range(0, T // 2, 1024):
            src0 = 2 * s0 - 1
            lo = max(0, -src0)
            hi_ = min(2051, T - src0)
            k.op("vector", "memset", [hstage.full], [], hstage.full, 0.0)
            k.dma("sync", hstage[:, lo:hi_],
                  (hd, hd[ci * 128:(ci + 1) * 128, src0 + lo:src0 + hi_]))
            for dstfm, fbase in ((low1, 0), (hi1, 4 * NCHUNK)):
                first = True
                for kk in range(4):
                    src_ap = bass.AP(hstage.h, 1 + (kk - 1),
                                     [[hstage.free, 128], [2, 1024]])
                    srcv = hstage.rawv(src_ap)
                    dv = dstfm.chunk(ci, s0, s0 + 1024)
                    fc = fcols[:, fbase + ci * 4 + kk:fbase + ci * 4 + kk + 1]
                    if first:
                        k.op("vector", "tensor_scalar", [dv], [srcv, fc],
                             dv, src_ap, fc, scalar2=None, op0=ALU.mult)
                        first = False
                    else:
                        k.op("vector", "scalar_tensor_tensor", [dv],
                             [srcv, fc, dv],
                             dv, src_ap, fc, dv, op0=ALU.mult, op1=ALU.add)
    emit_dwconv(k, low2, low1, fcols, 0, T // 4, 2, T // 2)
    emit_dwconv(k, hi2, low1, fcols, 4 * NCHUNK, T // 4, 2, T // 2, eng="gpsimd")
    for ci in range(NCHUNK):
        k.op("vector", "tensor_scalar", [qel.chunk(ci)], [low2.chunk(ci)],
             qel.chunk(ci), low2.chunk(ci), 0.0, scalar2=None, op0=ALU.min)
        k.op("scalar", "activation", [qel.chunk(ci)], [qel.chunk(ci)],
             qel.chunk(ci), qel.chunk(ci), AF.Exp)
        k.op("vector", "scalar_tensor_tensor", [qel.chunk(ci)],
             [low2.chunk(ci), qel.chunk(ci)],
             qel.chunk(ci), low2.chunk(ci), 0.0, qel.chunk(ci),
             op0=ALU.max, op1=ALU.add)
    NLT = T // 4 // 128
    for h in range(H):
        hc, hp = h // 2, (h % 2) * 64
        pKV = pE[0:64, 0:65]
        for nt in range(NLT):
            for srcfm, dstt in ((qel, ktok), (low2, vtok)):
                pt = pT[0 if srcfm is qel else 1]
                k.op("tensor", "transpose", [pt[0:128, 0:64]],
                     [srcfm.chunk(hc, nt * 128, (nt + 1) * 128,
                                  p0=hp, p1=hp + 64), idnb.full],
                     pt[0:128, 0:64],
                     srcfm.chunk(hc, nt * 128, (nt + 1) * 128,
                                 p0=hp, p1=hp + 64),
                     idnb[hp:hp + 64, hp:hp + 64])
                k.op("vector", "tensor_copy", [dstt[0:128, 0:DH]],
                     [pt[0:128, 0:64]],
                     dstt[0:128, 0:DH], pt[0:128, 0:64])
            k.op("gpsimd", "memset", [vtok[0:128, DH:DH + 1]], [],
                 vtok[0:128, DH:DH + 1], 1.0)
            k.op("tensor", "matmul", [pKV],
                 [ktok[0:128, 0:DH], vtok[0:128, 0:DH + 1]],
                 pKV, ktok[0:128, 0:DH], vtok[0:128, 0:DH + 1],
                 start=(nt == 0), stop=(nt == NLT - 1))
        k.op("vector", "tensor_copy", [kvm[hp:hp + 64, 0:65]], [pKV],
             kvm[hp:hp + 64, 0:65], pKV)
        for t0 in range(0, T // 4, 512):
            pNum = pF[0:65, 0:512]
            k.op("tensor", "matmul", [pNum],
                 [kvm[hp:hp + 64, 0:65],
                  qel.chunk(hc, t0, t0 + 512, p0=hp, p1=hp + 64)],
                 pNum, kvm[hp:hp + 64, 0:65],
                 qel.chunk(hc, t0, t0 + 512, p0=hp, p1=hp + 64),
                 start=True, stop=True)
            k.op("vector", "tensor_scalar", [recd[0:1, 0:512]], [pNum],
                 recd[0:1, 0:512], pF[64:65, 0:512], 1e-6,
                 scalar2=None, op0=ALU.add)
            k.op("vector", "reciprocal", [recd[0:1, 0:512]], [recd[0:1, 0:512]],
                 recd[0:1, 0:512], recd[0:1, 0:512])
            pB_ = ctx.psum("pC")[0:64, 0:512]
            k.op("tensor", "matmul", [pB_],
                 [ctx.ones_row_f[0:1, 0:64], recd[0:1, 0:512]],
                 pB_, ctx.ones_row_f[0:1, 0:64], recd[0:1, 0:512],
                 start=True, stop=True)
            k.op("vector", "tensor_copy", [recB.full], [pB_],
                 recB[0:64, 0:512], pB_)
            k.op("vector", "tensor_tensor",
                 [la.chunk(hc, t0, t0 + 512, p0=hp, p1=hp + 64)],
                 [pNum, recB.full],
                 la.chunk(hc, t0, t0 + 512, p0=hp, p1=hp + 64),
                 pF[0:64, 0:512], recB[0:64, 0:512], op=ALU.mult)
    emit_dwconv_up(k, low1, la, fcols, 2 * 4 * NCHUNK, T // 2, T // 4, True)
    emit_dwconv_up(k, low1, hi2, fcols, 3 * 4 * NCHUNK, T // 2, T // 4, False)
    sfull = FM(k, "sfull", D, 512, F32)
    for t0 in range(0, T, 512):
        pS = ctx.psum("pC")[0:1, 0:512]
        pQ = ctx.psum("pD")[0:1, 0:512]
        for ci in range(NCHUNK):
            sv = sfull.chunk(ci)
            k.dma("sync", hfin[:, 0:512],
                  (hd, hd[ci * 128:(ci + 1) * 128, t0:t0 + 512]))
            k.op("vector", "tensor_copy", [sv], [hfin[:, 0:512]],
                 sv, hfin[:, 0:512])
            for srcfm, fbase in ((low1, 2 * 4 * NCHUNK), (hi1, 3 * 4 * NCHUNK)):
                for par in range(2):
                    for kk in range(4):
                        num = par - 1 + kk
                        u_off = t0 // 2 + (num // 2 if num >= 0 else -((1 - num) // 2))
                        import math as _m
                        u_off = t0 // 2 + _m.floor(num / 2.0)
                        u_lo = max(0, -u_off)
                        u_hi = min(256, T // 2 - u_off)
                        if u_lo >= u_hi:
                            continue
                        dv_ap = bass.AP(sfull.tile.h,
                                        ci * sfull.ncols + par + 2 * u_lo,
                                        [[sfull.tile.free, 128],
                                         [2, u_hi - u_lo]])
                        dvv = sfull.tile.rawv(dv_ap)
                        src_ap = bass.AP(srcfm.tile.h,
                                         ci * srcfm.ncols + u_off + u_lo,
                                         [[srcfm.tile.free, 128],
                                          [1, u_hi - u_lo]])
                        srcv = srcfm.tile.rawv(src_ap)
                        fc = fcols[:, fbase + ci * 4 + kk:fbase + ci * 4 + kk + 1]
                        k.op("vector", "scalar_tensor_tensor", [dvv],
                             [srcv, fc, dvv],
                             dv_ap, src_ap, fc, dv_ap,
                             op0=ALU.mult, op1=ALU.add)
            k.op("scalar", "activation", [sq_f[ci % 2][:, 0:512]], [sv],
                 sq_f[ci % 2][:, 0:512], sv, AF.Square)
            k.op("tensor", "matmul", [pS], [ctx.ones_col_f[:, 0:1], sv],
                 pS, ctx.ones_col_f[:, 0:1], sv,
                 start=(ci == 0), stop=(ci == NCHUNK - 1))
            k.op("tensor", "matmul", [pQ],
                 [ctx.ones_col_f[:, 0:1], sq_f[ci % 2][:, 0:512]],
                 pQ, ctx.ones_col_f[:, 0:1], sq_f[ci % 2][:, 0:512],
                 start=(ci == 0), stop=(ci == NCHUNK - 1))
        k.op("vector", "tensor_scalar", [rows[0:1, 0:512]], [pS],
             rows[0:1, 0:512], pS, 1.0 / D, scalar2=None, op0=ALU.mult)
        k.op("vector", "tensor_scalar", [rows[0:1, 512:1024]], [pQ],
             rows[0:1, 512:1024], pQ, 1.0 / D, scalar2=None, op0=ALU.mult)
        mean = rows[0:1, 0:512]
        ex2 = rows[0:1, 512:1024]
        tmpr = rows[0:1, 1024:1536]
        k.op("vector", "tensor_tensor", [tmpr], [mean], tmpr, mean, mean,
             op=ALU.mult)
        k.op("vector", "tensor_tensor", [ex2], [ex2, tmpr], ex2, ex2, tmpr,
             op=ALU.subtract)
        k.op("scalar", "activation", [ex2], [ex2], ex2, ex2, AF.Ln,
             bias=np.float32(1e-5))
        k.op("scalar", "activation", [ex2], [ex2], ex2, ex2, AF.Exp,
             scale=np.float32(-0.5))
        emit_bcast_rows(k, ctx, mean, 512, aB)
        emit_bcast_rows(k, ctx, ex2, 512, bB)
        for ci in range(NCHUNK):
            sv = sfull.chunk(ci)
            k.op("vector", "tensor_tensor", [sv], [sv, aB[:, 0:512]],
                 sv, sv, aB[:, 0:512], op=ALU.subtract)
            k.op("vector", "tensor_tensor", [sv], [sv, bB[:, 0:512]],
                 sv, sv, bB[:, 0:512], op=ALU.mult)
            k.op("vector", "tensor_scalar", [sv],
                 [sv, gcol[:, ci:ci + 1], bcol[:, ci:ci + 1]],
                 sv, sv, gcol[:, ci:ci + 1], scalar2=bcol[:, ci:ci + 1],
                 op0=ALU.mult, op1=ALU.add)
            st = xstage[ci % 2]
            k.dma("scalar", st[:, 0:512],
                  (x_in, x_in[ci * 128:(ci + 1) * 128, t0:t0 + 512]))
            y = yst[ci % 2]
            k.op("vector", "tensor_tensor", [y.full], [sv, st[:, 0:512]],
                 y.full, sv, st[:, 0:512], op=ALU.add)
            k.dma("gpsimd", (y_out, y_out[ci * 128:(ci + 1) * 128,
                                          t0:t0 + 512]), y.full)

    k.seal()
    return k


def _l0attn_call(xT, Wq, Wk, Wv, Wo, bo):
    consts = get_cached_consts("l0attn_consts", lambda: l0attn_consts())
    kA = get_prog("l0attn_a", build_l0attn_a_prog, consts)
    import ml_dtypes
    inA = []
    for c in range(8):
        b, half = c // 2, c % 2
        (aqhi, aqlo), (akhi, aklo) = consts["per_half"][half]
        cols = slice(half * 512, half * 512 + 512)
        inA.append({
            "x": np.ascontiguousarray(xT[b]),
            "wq": np.ascontiguousarray(Wq[:, cols]),
            "wk": np.ascontiguousarray(Wk[:, cols]),
            "wv": np.ascontiguousarray(Wv[:, cols]),
            "aqhi": aqhi, "aqlo": aqlo, "akhi": akhi, "aklo": aklo,
        })
    resA = kA.run(inA)
    ctx_full = np.zeros((4, D, S), dtype=ml_dtypes.bfloat16)
    for c in range(8):
        b, half = c // 2, c % 2
        ctx_full[b, half * 512:half * 512 + 512, :] = resA[c]["ctxo"]
    kB = get_prog("l0attn_b", build_attnout_prog)
    inB = []
    for c in range(8):
        b, hs = c // 2, c % 2
        tok = slice(hs * 2048, hs * 2048 + 2048)
        inB.append({
            "x": np.ascontiguousarray(xT[b][:, tok]),
            "ctxb": np.ascontiguousarray(ctx_full[b][:, tok]),
            "w": Wo, "b": bo.reshape(D, 1),
        })
    resB = kB.run(inB)
    out = [np.zeros((D, S), np.float32) for _ in range(4)]
    for c in range(8):
        b, hs = c // 2, c % 2
        out[b][:, hs * 2048:(hs + 1) * 2048] = resB[c]["y"]
    return out


def _l1attn_call(xT, w_q, w_k, w_v, w_o):
    import ml_dtypes
    kA = get_prog("l1attn_a", build_l1attn_a_prog)
    inA = []
    for c in range(8):
        b, half = c // 2, c % 2
        cols = slice(half * 512, half * 512 + 512)
        inA.append({
            "x": np.ascontiguousarray(xT[b]),
            "wq": np.ascontiguousarray(w_q[:, cols]),
            "wk": np.ascontiguousarray(w_k[:, cols]),
            "wv": np.ascontiguousarray(w_v[:, cols]),
        })
    resA = kA.run(inA)
    ctx_full = np.zeros((4, D, S), dtype=ml_dtypes.bfloat16)
    for c in range(8):
        b, half = c // 2, c % 2
        ctx_full[b, half * 512:half * 512 + 512, :] = resA[c]["ctxo"]
    kB = get_prog("l1attn_b", build_attnout_prog, with_bias=False, coef=1.0)
    inB = []
    for c in range(8):
        b, hs = c // 2, c % 2
        tok = slice(hs * 2048, hs * 2048 + 2048)
        inB.append({
            "x": np.ascontiguousarray(xT[b][:, tok]),
            "ctxb": np.ascontiguousarray(ctx_full[b][:, tok]),
            "w": w_o,
        })
    resB = kB.run(inB)
    out = [np.zeros((D, S), np.float32) for _ in range(4)]
    for c in range(8):
        b, hs = c // 2, c % 2
        out[b][:, hs * 2048:(hs + 1) * 2048] = resB[c]["y"]
    return out


def _l1ffn_call(xT, w):
    import ml_dtypes
    kA = get_prog("l1ffn_a", build_l1ffn_a_prog)

    def perm_wq(half):
        cols = []
        for h in range(half * 8, half * 8 + 8):
            base = h * 64
            cols += list(range(base, base + 32))
            cols += list(range(base + 32, base + 64, 2))
            cols += list(range(base + 33, base + 64, 2))
        return np.ascontiguousarray(w["Wq"][:, cols])
    wlat = np.ascontiguousarray(w["Wkva"][:, 0:128])
    wx0 = np.ascontiguousarray(w["Wkva"][:, 128::2])
    wx1 = np.ascontiguousarray(w["Wkva"][:, 129::2])
    inA = []
    for c in range(8):
        b, half = c // 2, c % 2
        inA.append({
            "x": np.ascontiguousarray(xT[b]),
            "wq": perm_wq(half),
            "wkva_lat": wlat, "wkva_x0": wx0, "wkva_x1": wx1,
            "wkvb": np.ascontiguousarray(w["Wkvb"][:, half * 256:half * 256 + 256]),
            "wln1": w["w_ln1"].reshape(D, 1),
        })
    resA = kA.run(inA)
    mctx = np.zeros((4, 512, S), dtype=ml_dtypes.bfloat16)
    for c in range(8):
        b, half = c // 2, c % 2
        mctx[b, half * 256:half * 256 + 256, :] = resA[c]["ctxo"]
    wo_nope = np.ascontiguousarray(np.concatenate(
        [w["Wo"][h * 64:h * 64 + 32, :] for h in range(16)], axis=0))
    kB = get_prog("l1ffn_b", build_l1ffn_b_prog)
    inB = []
    for c in range(8):
        b, hs = c // 2, c % 2
        tok = slice(hs * 2048, hs * 2048 + 2048)
        inB.append({
            "x": np.ascontiguousarray(xT[b][:, tok]),
            "mctx": np.ascontiguousarray(mctx[b][:, tok]),
            "wo_nope": wo_nope,
            "wln2": w["w_ln2"].reshape(D, 1),
            "wgate": w["Wgate"], "ebias": w["expert_bias"].reshape(1, 8),
            "wg": w["Wg"], "wu": w["Wu"], "wd": w["Wd"],
            "swg": w["sWg"][0], "swu": w["sWu"][0], "swd": w["sWd"][0],
        })
    resB = kB.run(inB)
    out = [np.zeros((D, S), np.float32) for _ in range(4)]
    for c in range(8):
        b, hs = c // 2, c % 2
        out[b][:, hs * 2048:(hs + 1) * 2048] = resB[c]["y"]
    return out


def _l2ffn_call(xT, wdict):
    k = get_prog("l2ffn_%x" % id(wdict["sgT"]), build_l2ffn_prog, wdict)
    ins = [{"x": np.ascontiguousarray(xT[c // 2][:, (c % 2) * 2048:(c % 2 + 1) * 2048])}
           for c in range(8)]
    res = k.run(ins)
    out = [np.zeros((D, S), np.float32) for _ in range(4)]
    for c in range(8):
        b, hs = c // 2, c % 2
        out[b][:, hs * 2048:(hs + 1) * 2048] = res[c]["y"]
    return out


def _l2attn_call(xT, h_filter, g_filter, h_synth, g_synth, gamma, beta):
    k = get_prog("l2attn", build_l2attn_prog)
    fc = np.zeros((128, 4 * 8 * 4), np.float32)
    for fi, f in enumerate((h_filter, g_filter, h_synth, g_synth)):
        for ci in range(8):
            for kk in range(4):
                fc[:, fi * 32 + ci * 4 + kk] = f[kk, ci * 128:(ci + 1) * 128]
    ins = []
    for c in range(8):
        b = c // 2
        ins.append({
            "x": np.ascontiguousarray(xT[b]),
            "filts": fc,
            "gcol": gamma.reshape(D, 1),
            "bcol": beta.reshape(D, 1),
        })
    res = k.run(ins)
    out = [np.zeros((D, S), np.float32) for _ in range(4)]
    for c in range(8):
        b, hs = c // 2, c % 2
        out[b][:, hs * 2048:(hs + 1) * 2048] = \
            res[c]["y"][:, hs * 2048:(hs + 1) * 2048]
    return out


def _l3attn_call(xT, W_qkv, b_qkv, W_out, b_out):
    H, DK = 16, 64
    qcols = [h * 192 + d for h in range(H) for d in range(DK)]
    kcols = [h * 192 + 64 + d for h in range(H) for d in range(DK)]
    vcols = [h * 192 + 128 + d for h in range(H) for d in range(DK)]
    k = get_prog("l3attn", build_l3attn_prog)
    ins = []
    for c in range(8):
        b, hs = c // 2, c % 2
        tok = slice(hs * 2048, hs * 2048 + 2048)
        ins.append({
            "x": np.ascontiguousarray(xT[b][:, tok]),
            "wq": np.ascontiguousarray(W_qkv[:, qcols]),
            "wk": np.ascontiguousarray(W_qkv[:, kcols]),
            "wv": np.ascontiguousarray(W_qkv[:, vcols]),
            "wout": W_out,
            "bq": np.ascontiguousarray(b_qkv[qcols]).reshape(1, -1),
            "bk": np.ascontiguousarray(b_qkv[kcols]).reshape(1, -1),
            "bv": np.ascontiguousarray(b_qkv[vcols]).reshape(1, -1),
            "bout": b_out.reshape(D, 1),
        })
    res = k.run(ins)
    out = [np.zeros((D, S), np.float32) for _ in range(4)]
    for c in range(8):
        b, hs = c // 2, c % 2
        out[b][:, hs * 2048:(hs + 1) * 2048] = res[c]["y"]
    return out


_CONST_CACHE = {}


def get_cached_consts(name, fn):
    if name not in _CONST_CACHE:
        _CONST_CACHE[name] = fn()
    return _CONST_CACHE[name]


def kernel(**inputs):
    i = inputs
    x = np.asarray(i["L0_attn_x"], dtype=np.float32)
    xT = [np.ascontiguousarray(x[b].T) for b in range(4)]

    def lwin_w(pfx):
        return {
            "wqkv": [np.ascontiguousarray(i[pfx + "_wqkv"][l]) for l in range(2)],
            "qs": [i[pfx + "_q_scale"][l] for l in range(2)],
            "ks": [i[pfx + "_k_scale"][l] for l in range(2)],
            "wo": [np.ascontiguousarray(i[pfx + "_wo"][l]) for l in range(2)],
            "lg": [i[pfx + "_ln_g"][l] for l in range(2)],
            "lb": [i[pfx + "_ln_b"][l] for l in range(2)],
            "w1": [np.ascontiguousarray(i[pfx + "_w1"][l]) for l in range(2)],
            "b1": [i[pfx + "_b1"][l] for l in range(2)],
            "w2": [np.ascontiguousarray(i[pfx + "_w2"][l]) for l in range(2)],
            "b2": [i[pfx + "_b2"][l] for l in range(2)],
        }
    l0w = lwin_w("L0_ffn")
    l3w = lwin_w("L3_ffn")
    l1w = {n: i["L1_ffn_" + n] for n in
           ("w_ln1", "Wq", "Wkva", "Wkvb", "Wo", "w_ln2", "Wgate",
            "expert_bias", "Wg", "Wu", "Wd", "sWg", "sWu", "sWd")}
    l2fw = l2ffn_host_weights(i["L2_ffn_gate_w"], i["L2_ffn_shared_gate"],
                              i["L2_ffn_shared_up"], i["L2_ffn_shared_down"],
                              i["L2_ffn_r_gate"], i["L2_ffn_r_up"],
                              i["L2_ffn_r_down"])

    xT = run_lwin_call(xT, l0w, coef=0.5)
    xT = _l0attn_call(xT, i["L0_attn_Wq"], i["L0_attn_Wk"], i["L0_attn_Wv"],
                      i["L0_attn_Wo"], i["L0_attn_bo"])
    xT = run_lwin_call(xT, l0w, coef=0.5)
    xT = _l1ffn_call(xT, l1w)
    xT = _l1attn_call(xT, i["L1_attn_w_q"], i["L1_attn_w_k"],
                      i["L1_attn_w_v"], i["L1_attn_w_o"])
    xT = _l1ffn_call(xT, l1w)
    xT = _l2ffn_call(xT, l2fw)
    xT = _l2attn_call(xT, i["L2_attn_h_filter"], i["L2_attn_g_filter"],
                      i["L2_attn_h_synth"], i["L2_attn_g_synth"],
                      i["L2_attn_norm_gamma"], i["L2_attn_norm_beta"])
    xT = _l2ffn_call(xT, l2fw)
    xT = run_lwin_call(xT, l3w, coef=0.5)
    xT = _l3attn_call(xT, i["L3_attn_W_qkv"], i["L3_attn_b_qkv"],
                      i["L3_attn_W_out"], i["L3_attn_b_out"])
    xT = run_lwin_call(xT, l3w, coef=0.5)
    return np.stack([xT[b].T for b in range(4)]).astype(np.float32)
